# version 41
# baseline (speedup 1.0000x reference)
import math

import numpy as np
import ml_dtypes

import concourse.bass as bass
import concourse.mybir as mybir
from concourse.bass_utils import run_bass_kernel_spmd

F32 = mybir.dt.float32
BF16 = mybir.dt.bfloat16
ALU = mybir.AluOpType
AF = mybir.ActivationFunctionType
AX = mybir.AxisListType
BF = ml_dtypes.bfloat16

C_RNE = float(np.float32(1.5 * 2.0 ** 23))
INV_SQRT8 = float(np.float32(1.0 / math.sqrt(8.0)))
QMAXI = 32767.0
EPS_LN = 1e-5


def mkcfg(S=4096):
    c = {}
    c["S"] = S
    c["B"] = 4
    c["D"] = 1024
    c["H"] = 16
    c["HD"] = 64
    c["I"] = 3072
    c["NTB"] = S // 128
    c["NTM"] = c["NTB"] // 2
    c["TOK"] = S // 2
    c["NCD"] = c["D"] // 128
    c["NCI"] = c["I"] // 128
    c["PAIRS"] = c["H"] // 2
    c["QBS"] = min(512, c["TOK"])
    c["NQB"] = c["TOK"] // c["QBS"]
    c["KCH"] = S // 128
    c["GSZ"] = min(512, c["TOK"])
    c["NTG"] = c["TOK"] // c["GSZ"]
    return c


class Prog:
    def __init__(self, nc):
        self.nc = nc
        self.eng = {"PE": nc.tensor, "DV": nc.vector, "AC": nc.scalar,
                    "PL": nc.gpsimd, "SY": nc.sync}
        self.sems = {}
        self.cnt = {}
        self.lastw = {}

    def _sem(self, name):
        if name not in self.sems:
            self.sems[name] = self.nc.alloc_semaphore(name)
            self.cnt[name] = 0
        return self.sems[name]

    def op(self, engname, inst):
        s = self._sem(engname)
        if engname in ("DV", "AC", "PL") and self.cnt[engname] > 0:
            inst._wait_ge(s, self.cnt[engname])
        inst.then_inc(s, 1)
        self.cnt[engname] += 1
        return self.cnt[engname]

    def PE(self, m, *a, **k):
        return self.op("PE", getattr(self.nc.tensor, m)(*a, **k))

    def DV(self, m, *a, **k):
        return self.op("DV", getattr(self.nc.vector, m)(*a, **k))

    def AC(self, m, *a, **k):
        return self.op("AC", getattr(self.nc.scalar, m)(*a, **k))

    def PL(self, m, *a, **k):
        return self.op("PL", getattr(self.nc.gpsimd, m)(*a, **k))

    def dma(self, stream, dst, src, q="SY", **kw):
        s = self._sem(stream)
        self.eng[q].dma_start(dst, src, **kw).then_inc(s, 16)
        self.cnt[stream] += 16
        return self.cnt[stream]

    def dmat(self, stream, dst, src, q="SY"):
        s = self._sem(stream)
        self.eng[q].dma_start(dst, src, transpose=True).then_inc(s, 16)
        self.cnt[stream] += 16
        return self.cnt[stream]

    def wait(self, engname, dep, val=None):
        if dep not in self.cnt:
            return
        val = self.cnt[dep] if val is None else val
        if val <= 0:
            return
        key = (engname, dep)
        if self.lastw.get(key, -1) >= val:
            return
        self.eng[engname].wait_ge(self._sem(dep), val)
        self.lastw[key] = val


def build_core(cfg):
    S, B, D, H, HD, I = cfg["S"], cfg["B"], cfg["D"], cfg["H"], cfg["HD"], cfg["I"]
    NTB, NTM, TOK = cfg["NTB"], cfg["NTM"], cfg["TOK"]
    NCD, NCI, PAIRS = cfg["NCD"], cfg["NCI"], cfg["PAIRS"]
    QBS, NQB, KCH = cfg["QBS"], cfg["NQB"], cfg["KCH"]
    GSZ, NTG = cfg["GSZ"], cfg["NTG"]

    nc = bass.Bass("TRN2", target_bir_lowering=False, num_devices=8)
    p = Prog(nc)

    xb = nc.dram_tensor("xb", [S, D], BF16, kind="ExternalInput").ap()
    wqkvT = nc.dram_tensor("wqkvT", [D, 3 * D], BF16, kind="ExternalInput").ap()
    woT = nc.dram_tensor("woT", [D, D], BF16, kind="ExternalInput").ap()
    bqkv = nc.dram_tensor("bqkv", [1, 3 * D], BF16, kind="ExternalInput").ap()
    bo = nc.dram_tensor("bo", [1, D], BF16, kind="ExternalInput").ap()
    wg_n = nc.dram_tensor("wg", [I, D], F32, kind="ExternalInput").ap()
    wu_n = nc.dram_tensor("wu", [I, D], F32, kind="ExternalInput").ap()
    wd_n = nc.dram_tensor("wd", [D, I], F32, kind="ExternalInput").ap()
    tabc = nc.dram_tensor("tabc", [S, HD], F32, kind="ExternalInput").ap()
    tabs = nc.dram_tensor("tabs", [S, HD], F32, kind="ExternalInput").ap()
    y = nc.dram_tensor("y", [TOK, D], F32, kind="ExternalOutput").ap()

    dscr = lambda nm, shp, dt: nc.dram_tensor(nm, shp, dt, kind="Internal").ap()
    xn_scr = dscr("xn_scr", [S, D], BF16)
    q_scr = dscr("q_scr", [TOK, D], BF16)
    k_scr = dscr("k_scr", [S, D], BF16)
    v_scr = dscr("v_scr", [S, D], BF16)
    wdq_scr = dscr("wdq_scr", [I, D], BF16)
    wdqg_scr = dscr("wdqg_scr", [I, D], BF16)
    wddq_scr = dscr("wddq_scr", [D, I], BF16)
    up_scr = dscr("up_scr", [I, TOK], F32)
    gate_scr = dscr("gate_scr", [I, TOK], F32)
    t1_scr = dscr("t1_scr", [I, TOK], F32)
    t2_scr = dscr("t2_scr", [NTM, I, 128], F32)
    s_scr = dscr("s_scr", [TOK, D], BF16)
    red_scr = dscr("red_scr", [64, 128], F32)

    ones_row = nc.inline_tensor(np.ones((1, 128), BF), "c_ones_row").ap()
    id64c = nc.inline_tensor(np.eye(64, dtype=BF), "c_id64").ap()
    onesP64c = nc.inline_tensor(np.ones((128, 64), np.float32), "c_onesP64").ap()
    onesB64c = nc.inline_tensor(np.ones((128, 64), BF), "c_onesB64").ap()

    sb = lambda nm, shp, dt: nc.alloc_sbuf_tensor(nm, shp, dt).ap()
    ones_sb = sb("ones_sb", [1, 128], BF16)
    id64_sb = sb("id64_sb", [64, 64], BF16)
    onesP64 = sb("onesP64", [128, 64], F32)
    onesB64 = sb("onesB64", [128, 64], BF16)
    bqkv_sb = sb("bqkv_sb", [1, 3 * D], BF16)
    bo_sb = sb("bo_sb", [1, D], BF16)
    stat = sb("stat", [128, 8 * NTB], F32)
    nrm = sb("nrm", [128, 48], F32)
    scal = sb("scal", [1, 64], F32)
    scbc = sb("scbc", [128, 16], F32)
    sA = sb("sA", [128, 1024], F32)
    sB = sb("sB", [128, 1024], F32)
    sC = sb("sC", [128, 1024], F32)
    sD = sb("sD", [128, 512], F32)
    sE = sb("sE", [128, 512], F32)
    sF = sb("sF", [128, 64], F32)
    bfA = sb("bfA", [128, 1024], BF16)
    bfB = sb("bfB", [128, 1024], BF16)
    bfC = sb("bfC", [128, 1024], BF16)
    wnat = sb("wnat", [128, 2048], F32)
    dqstage = sb("dqstage", [128, 1024], BF16)
    rr = sb("rr", [128, 1024], F32)

    R1 = max(NCD * S * 2,
             2 * (S + TOK) * 2 + 2 * KCH * 130 * 2 + 2 * 2 * QBS * 2,
             NCI * D * 2 + NCI * 128 * 4 + 2 * 2048)
    R2 = max(NCD * D * 2 + NCD * 2 * D * 2,
             NCD * I * 2,
             NCD * D * 2 + 8 * min(1024, TOK) * 4,
             NCD * D * 2 + 2 * NCI * 128 * 4)
    R3 = max(NCD * TOK * 2, 3 * D * 2)
    R4 = max(2 * NTB * HD * 4,
             4 * 2 * QBS * 2 + 2048 * 4,
             2 * NCI * 128 * 2)
    AR = ((R1 + 63) // 64 * 64) + ((R2 + 63) // 64 * 64) \
        + ((R3 + 63) // 64 * 64) + ((R4 + 63) // 64 * 64)
    arena = nc.alloc_sbuf_tensor("arena", [128, AR], mybir.dt.uint8).ap()
    o1 = 0
    o2 = o1 + (R1 + 63) // 64 * 64
    o3 = o2 + (R2 + 63) // 64 * 64
    o4 = o3 + (R3 + 63) // 64 * 64

    def carve(off, nbytes, dt, shape):
        ap = arena[:, off:off + nbytes].bitcast(dt)
        if len(shape) == 2:
            return ap
        if len(shape) == 3:
            return ap.rearrange("p (a b) -> p a b", b=shape[2])
        raise ValueError(shape)

    xnT = carve(o1, NCD * S * 2, BF16, [128, NCD, S])
    ktq = [carve(o1 + i * (S + TOK) * 2, (S + TOK) * 2, BF16,
                 [128, S + TOK]) for i in range(2)]
    _vo = o1 + 2 * (S + TOK) * 2
    vpair = [carve(_vo + i * KCH * 130 * 2, KCH * 130 * 2, BF16,
                   [128, KCH, 130]) for i in range(2)]
    _po = _vo + 2 * KCH * 130 * 2
    pt = [carve(_po + i * 2 * QBS * 2, 2 * QBS * 2, BF16,
                [128, 2 * QBS]) for i in range(2)]
    wddqT = carve(o1, NCI * D * 2, BF16, [128, NCI, D])

    wvT = carve(o2, NCD * D * 2, BF16, [128, NCD, D])
    wqkT = carve(o2 + NCD * D * 2, NCD * 2 * D * 2, BF16, [128, NCD, 2 * D])
    wdqT = carve(o2, NCD * I * 2, BF16, [128, NCD, I])
    woT_sb = carve(o2, NCD * D * 2, BF16, [128, NCD, D])
    _t2o = o2 + NCD * D * 2
    t2sl = [carve(_t2o + i * (NCI * 128 * 4), NCI * 128 * 4, F32,
                  [128, NCI, 128]) for i in range(2)]
    t2sl.append(carve(o1 + NCI * D * 2, NCI * 128 * 4, F32, [128, NCI, 128]))
    osl = [carve(o4 + i * (NCI * 128 * 2), NCI * 128 * 2,
                 BF16, [128, NCI, 128]) for i in range(2)]

    x_q = carve(o3, NCD * TOK * 2, BF16, [128, NCD, TOK])
    o_attn = carve(o3, NCD * TOK * 2, BF16, [128, NCD, TOK])
    xstg = carve(o3, D * 2, BF16, [128, D])
    xstg2 = carve(o3 + D * 2, D * 2, BF16, [128, D])
    xstg3 = carve(o3 + 2 * D * 2, D * 2, BF16, [128, D])

    ctab = carve(o4, NTB * HD * 4, F32, [128, NTB, HD])
    stab = carve(o4 + NTB * HD * 4, NTB * HD * 4, F32, [128, NTB, HD])
    oring = [carve(o4 + i * 2 * QBS * 2, 2 * QBS * 2, BF16, [128, 2 * QBS])
             for i in range(4)]
    nsc = carve(o4 + 4 * 2 * QBS * 2, 2048 * 4, F32, [128, 2048])
    PW = min(1024, TOK)
    _p5 = o2 + NCD * D * 2
    ph5_in = [carve(_p5 + i * PW * 4, PW * 4, F32, [128, PW]) for i in range(4)]
    ph5_out = [carve(_p5 + (4 + i) * PW * 4, PW * 4, F32, [128, PW]) for i in range(4)]
    ph5_t1 = ph5_in
    ph5_up = ph5_out
    ph5_t2 = ph5_in

    ps = lambda nm, shp: nc.alloc_psum_tensor(nm, shp, F32).ap()
    P0 = ps("P0", [128, 1024])
    P1 = ps("P1", [128, 1024])
    P2 = ps("P2", [128, 1024])
    P34 = ps("P34", [128, 1024])

    rrow = [0]

    def preduce(src, ncols, dst_scal_off, dv_tok=None):
        base = rrow[0]
        rrow[0] += ncols
        p.wait("DV", "DV")
        p.wait("SY", "DV", dv_tok)
        for j in range(ncols):
            p.dma("prs", red_scr[base + j:base + j + 1, :],
                  src[:, j:j + 1])
        p.wait("SY", "prs")
        done = 0
        prev_dv = None
        while done < ncols:
            if prev_dv is not None:
                p.wait("SY", "DV", prev_dv)
            nb = min(8, ncols - done)
            tok = p.dma("prl", sA[0:1, 0:nb * 128],
                        red_scr[base + done:base + done + nb, :]
                        .rearrange("k p -> (k p)").unsqueeze(0))
            p.wait("DV", "prl", tok)
            dv = p.DV("tensor_reduce",
                      scal[0:1, dst_scal_off + done:dst_scal_off + done + nb]
                      .unsqueeze(2),
                      sA[0:1, 0:nb * 128].rearrange("o (k p) -> o k p", p=128),
                      AX.X, ALU.max)
            p.wait("DV", "DV", dv)
            prev_dv = dv
            done += nb

    def bcast_scal(src_off, n, dst_ap, stream, dv_tok=None):
        base = rrow[0]
        rrow[0] += 1
        p.wait("SY", "DV", dv_tok)
        p.dma("prs", red_scr[base:base + 1, 0:n], scal[0:1, src_off:src_off + n])
        p.wait("SY", "prs")
        p.dma(stream, dst_ap, red_scr[base:base + 1, 0:n].broadcast_to((128, n)),
              allow_slow_non_contiguous=True)

    p.dma("ld0", ones_sb, ones_row)
    p.dma("ld0", id64_sb, id64c)
    p.dma("ld0", onesP64, onesP64c)
    p.dma("ld0", onesB64, onesB64c)
    p.dma("ld0", bqkv_sb, bqkv)
    p.dma("ld0", bo_sb, bo)

    _dq_last_store = [None]
    _dq_d6h = []

    def make_dq(w_in, rows, cols, out_scr, stream):
        ntile = rows // 128
        SLW = min(1024, cols)
        nblk = SLW // 32
        st = {"idx": 0, "d4": {}, "st": {}, "ld": {}}
        total = ntile * (cols // SLW)

        def step():
            idx = st["idx"]
            if idx > total:
                return False
            if idx < total:
                r, cs = divmod(idx, cols // SLW)
                cs *= SLW
                wt = wnat[:, 0:1024] if idx % 2 == 0 else wnat[:, 1024:2048]
                if idx >= 2:
                    p.wait("PL", "PL", st["d4"][idx - 2])
                st["ld"][idx] = p.dma(stream + str(idx % 2), wt[:, 0:SLW],
                                      w_in[r * 128:(r + 1) * 128, cs:cs + SLW],
                                      q="PL")
            if idx >= 1:
                jdx = idx - 1
                r, cs = divmod(jdx, cols // SLW)
                cs *= SLW
                wt = wnat[:, 0:1024] if jdx % 2 == 0 else wnat[:, 1024:2048]
                sEb = sE[:, 0:nblk] if jdx % 2 == 0 else sE[:, nblk:2 * nblk]
                sFb = sF[:, 0:nblk] if jdx % 2 == 0 else sF[:, nblk:2 * nblk]
                p.wait("DV", stream + str(jdx % 2), st["ld"][jdx])
                if len(_dq_d6h) >= 2:
                    p.wait("DV", "PL", _dq_d6h[-2])
                p.DV("tensor_reduce", sEb.unsqueeze(2),
                     wt[:, 0:SLW].rearrange("p (b e) -> p b e", e=32),
                     AX.X, ALU.max, apply_absolute_value=True)
                p.DV("tensor_scalar", sEb, sEb,
                     1.0 / 7.0, 1e-12, ALU.mult, ALU.max)
                d3 = p.DV("reciprocal", sFb, sEb)
                p.wait("PL", "DV", d3)
                d4 = p.PL("tensor_tensor",
                          rr[:, 0:SLW].rearrange("p (b e) -> p b e", e=32),
                          wt[:, 0:SLW].rearrange("p (b e) -> p b e", e=32),
                          sFb.unsqueeze(2)
                          .broadcast_to((128, nblk, 32)), ALU.mult)
                st["d4"][jdx] = d4
                p.wait("DV", "PL", d4)
                d5 = p.DV("tensor_scalar", rr[:, 0:SLW], rr[:, 0:SLW],
                          C_RNE, C_RNE, ALU.add, ALU.subtract)
                p.wait("PL", "DV", d5)
                if _dq_last_store[0] is not None:
                    p.wait("PL", _dq_last_store[0][0], _dq_last_store[0][1])
                d6 = p.PL("tensor_tensor",
                          dqstage[:, 0:SLW].rearrange("p (b e) -> p b e", e=32),
                          rr[:, 0:SLW].rearrange("p (b e) -> p b e", e=32),
                          sEb.unsqueeze(2)
                          .broadcast_to((128, nblk, 32)), ALU.mult)
                p.wait("PL", "PL", d6)
                _dq_d6h.append(d6)
                st["st"][jdx] = p.dma("stwq" + stream,
                                      out_scr[r * 128:(r + 1) * 128,
                                              cs:cs + SLW],
                                      dqstage[:, 0:SLW], q="PL")
                _dq_last_store[0] = ("stwq" + stream, st["st"][jdx])
            st["idx"] += 1
            return True
        return step

    _dqs = [make_dq(wu_n, I, D, wdq_scr, "ldwu"),
            make_dq(wg_n, I, D, wdqg_scr, "ldwg"),
            make_dq(wd_n, D, I, wddq_scr, "ldwd")]

    def dq_step(n=1):
        k = 0
        while k < n and _dqs:
            if not _dqs[0]():
                _dqs.pop(0)
                continue
            k += 1

    def dq_finish():
        while _dqs:
            dq_step(4)

    stx = {}
    ph1_ac = {}
    ph1_sum = {}
    ph1_ld = {}
    xbufs = [bfA, bfB, sC[:, 0:512].bitcast(BF16), sC[:, 512:1024].bitcast(BF16)]

    def ph1_A0(i):
        xt = xbufs[i % 4]
        if i >= 4:
            p.wait("SY", "AC", stx[i - 4][0])
        ph1_ld[i] = p.dma(f"ldx{i % 4}", xt[:, 0:D],
                          xb[i * 128:(i + 1) * 128, :])

    def ph1_A1(i):
        xt = xbufs[i % 4]
        sq = sA if i % 2 == 0 else sB
        p.wait("AC", f"ldx{i % 4}", ph1_ld[i])
        p.AC("activation", sq[:, 0:D], xt[:, 0:D], AF.Square,
             accum_out=stat[:, NTB + i:NTB + i + 1])
        ph1_sum[i] = p.AC("activation", sq[:, 0:D], xt[:, 0:D], AF.Copy,
                          accum_out=stat[:, i:i + 1])

    def ph1_A2(i):
        mci = stat[:, 2 * NTB + i:2 * NTB + i + 1]
        vci = stat[:, 3 * NTB + i:3 * NTB + i + 1]
        rsi = stat[:, 4 * NTB + i:4 * NTB + i + 1]
        p.wait("DV", "AC", ph1_sum[i])
        p.DV("tensor_scalar", mci, stat[:, i:i + 1], -1.0 / D, None, ALU.mult)
        p.DV("tensor_mul", vci, mci, mci)
        p.DV("scalar_tensor_tensor", vci, stat[:, NTB + i:NTB + i + 1],
             1.0 / D, vci, ALU.mult, ALU.subtract)
        dvv = p.DV("tensor_scalar", vci, vci, EPS_LN, None, ALU.add)
        p.wait("AC", "DV", dvv)
        ph1_ac[i] = p.AC("activation", rsi, vci, AF.Sqrt)

    def ph1_B(i):
        xt = xbufs[i % 4]
        mci = stat[:, 2 * NTB + i:2 * NTB + i + 1]
        rsi = stat[:, 4 * NTB + i:4 * NTB + i + 1]
        bci = stat[:, 5 * NTB + i:5 * NTB + i + 1]
        p.wait("DV", "AC", ph1_ac[i])
        p.DV("reciprocal", rsi, rsi)
        dvb = p.DV("tensor_mul", bci, rsi, mci)
        xo = bfC if i % 2 == 0 else xstg
        p.wait("AC", "DV", dvb)
        if i >= 2:
            p.wait("AC", "stxn%d" % (i % 2), stx[i - 2][1])
        dvx = p.AC("activation", xo[:, 0:D], xt[:, 0:D], AF.Identity,
                   bias=bci, scale=rsi)
        if i >= 1:
            pxo = bfC if (i - 1) % 2 == 0 else xstg
            p.wait("DV", "AC", stx[i - 1][0])
            p.DV("tensor_reduce", stat[:, 6 * NTB + i - 1:6 * NTB + i]
                 .unsqueeze(2), pxo[:, 0:D].unsqueeze(1), AX.X, ALU.max,
                 apply_absolute_value=True)
        p.wait("SY", "AC", dvx)
        st = p.dma("stxn%d" % (i % 2), xn_scr[i * 128:(i + 1) * 128, :],
                   xo[:, 0:D])
        stx[i] = (dvx, st)

    vtok = {}
    xnt_tok = {}

    TBS = 4 if NTB >= 4 else NTB
    TBW = TBS * 128

    def tblock_T(tb):
        p.wait("SY", "stxn0", stx[TBS * tb + TBS - 2][1])
        p.wait("SY", "stxn1", stx[TBS * tb + TBS - 1][1])
        toks = []
        for c in range(NCD):
            snm = "ldxnT%d" % (2 * (tb % 2) + c % 2)
            tok = p.dmat(snm, xnT[:, c, tb * TBW:(tb + 1) * TBW],
                         xn_scr[tb * TBW:(tb + 1) * TBW,
                                c * 128:(c + 1) * 128])
            toks.append((snm, tok))
        xnt_tok[tb] = toks[-2:]

    vpe = {}

    def ph2v_mm(i):
        pv = P0 if i % 2 == 0 else P1
        if i >= 2:
            p.wait("PE", "DV", vtok[i - 2][0])
        for snm, tok in xnt_tok[i // TBS]:
            p.wait("PE", snm, tok)
        for c in range(NCD):
            p.PE("matmul", pv[:, 0:512], xnT[:, c, i * 128:(i + 1) * 128],
                 wvT[:, c, 0:512], start=(c == 0), stop=False)
            p.PE("matmul", pv[:, 512:1024], xnT[:, c, i * 128:(i + 1) * 128],
                 wvT[:, c, 512:1024], start=(c == 0), stop=False)
        p.PE("matmul", pv[:, 0:512], ones_sb[0:1, :],
             bqkv_sb[0:1, 2 * D:2 * D + 512], start=False, stop=True)
        vpe[i] = p.PE("matmul", pv[:, 512:1024], ones_sb[0:1, :],
                      bqkv_sb[0:1, 2 * D + 512:3 * D], start=False, stop=True)

    def ph2v_ev(i):
        pv = P0 if i % 2 == 0 else P1
        p.wait("DV", "PE", vpe[i])
        vbuf = xstg2 if i % 2 == 0 else xstg3
        if i >= 2:
            p.wait("DV", "stv%d" % (i % 2), vtok[i - 2][1])
        dvv = p.DV("tensor_copy", vbuf[:, 0:D], pv[:, 0:1024])
        p.wait("SY", "DV", dvv)
        st = p.dma("stv%d" % (i % 2), v_scr[i * 128:(i + 1) * 128, :],
                   vbuf[:, 0:D])
        vtok[i] = (dvv, st)

    def ph2v_chunk(i):
        ph2v_mm(i)
        if i >= 1:
            ph2v_ev(i - 1)

    for c in range(NCD):
        p.dma("ldwv", wvT[:, c, :], wqkvT[c * 128:(c + 1) * 128, 2 * D:3 * D])
    p.wait("PE", "ldwv")
    p.wait("PE", "ld0")
    NTBL = NTB // TBS
    ph1_A0(0)
    if NTB > 1:
        ph1_A0(1)
    for s in range(NTB + 2):
        if s >= 2:
            b = s - 2
            ph1_B(b)
            dq_step(1)
        if s + 2 < NTB:
            ph1_A0(s + 2)
        if s >= 2:
            b = s - 2
            if b % TBS == TBS - 1:
                tblock_T(b // TBS)
            if b % TBS == 1 and b >= TBS:
                for j in range(TBS * (b // TBS - 1), TBS * (b // TBS)):
                    ph2v_chunk(j)
                    dq_step(1)
        if 1 <= s <= NTB:
            ph1_A2(s - 1)
        if s < NTB:
            ph1_A1(s)
    p.wait("DV", "AC", stx[NTB - 1][0])
    p.DV("tensor_reduce", stat[:, 7 * NTB - 1:7 * NTB].unsqueeze(2),
         (bfC if (NTB - 1) % 2 == 0 else xstg)[:, 0:D].unsqueeze(1),
         AX.X, ALU.max, apply_absolute_value=True)
    for j in range(TBS * (NTBL - 1), NTB):
        ph2v_chunk(j)
        dq_step(1)
    ph2v_ev(NTB - 1)

    p.dma("ldtab", ctab, tabc.rearrange("(t p) d -> p t d", p=128))
    p.dma("ldtab", stab, tabs.rearrange("(t p) d -> p t d", p=128))
    for c in range(NCD):
        p.dma("ldwqk", wqkT[:, c, 0:D], wqkvT[c * 128:(c + 1) * 128, 0:D])
        p.dma("ldwqk", wqkT[:, c, D:2 * D], wqkvT[c * 128:(c + 1) * 128, D:2 * D])
    p.wait("PE", "ldwqk")
    p.wait("PE", "DV", vtok[NTB - 1][0])
    p.wait("DV", "ldtab")
    p.wait("DV", "stv0")
    p.wait("DV", "stv1")
    dv = p.DV("memset", nrm[:, 0:32], 0.0)
    p.wait("DV", "DV", dv)
    qk_dv = {}
    stk_tok = {}
    stq_tok = {}

    def rope(psrc, dst_bf, pe_tok):
        p.wait("DV", "PE", pe_tok)
        v3 = psrc.rearrange("p (h two q) -> p h two q", two=2, q=HD // 2)
        vswap = v3[:, :, ::-1, :]
        cosb = ctab[:, rope_i[0], :].rearrange("p (two q) -> p two q", two=2) \
            .unsqueeze(1).broadcast_to((128, H, 2, HD // 2))
        sinb = stab[:, rope_i[0], :].rearrange("p (two q) -> p two q", two=2) \
            .unsqueeze(1).broadcast_to((128, H, 2, HD // 2))
        d1 = p.DV("tensor_tensor", sA[:, 0:D].rearrange(
            "p (h two q) -> p h two q", two=2, q=HD // 2), v3, cosb, ALU.mult)
        p.wait("DV", "DV", d1)
        d2 = p.DV("tensor_tensor", sB[:, 0:D].rearrange(
            "p (h two q) -> p h two q", two=2, q=HD // 2), vswap, sinb, ALU.mult)
        p.wait("DV", "DV", d2)
        d3 = p.DV("tensor_add", dst_bf[:, 0:D], sA[:, 0:D], sB[:, 0:D])
        p.wait("DV", "DV", d3)
        return d3

    rope_i = [0]
    for i in range(NTB):
        rope_i[0] = i
        mine = i < NTM
        pk = P0 if i % 2 == 0 else P1
        pq = P2 if i % 2 == 0 else P34
        if i >= 2:
            p.wait("PE", "DV", qk_dv[i - 2])
        for c in range(NCD):
            p.PE("matmul", pk[:, 0:512], xnT[:, c, i * 128:(i + 1) * 128],
                 wqkT[:, c, D:D + 512], start=(c == 0), stop=False)
            p.PE("matmul", pk[:, 512:1024], xnT[:, c, i * 128:(i + 1) * 128],
                 wqkT[:, c, D + 512:2 * D], start=(c == 0), stop=False)
            if mine:
                p.PE("matmul", pq[:, 0:512], xnT[:, c, i * 128:(i + 1) * 128],
                     wqkT[:, c, 0:512], start=(c == 0), stop=False)
                p.PE("matmul", pq[:, 512:1024], xnT[:, c, i * 128:(i + 1) * 128],
                     wqkT[:, c, 512:1024], start=(c == 0), stop=False)
        p.PE("matmul", pk[:, 0:512], ones_sb[0:1, :], bqkv_sb[0:1, D:D + 512],
             start=False, stop=True)
        pek = p.PE("matmul", pk[:, 512:1024], ones_sb[0:1, :],
                   bqkv_sb[0:1, D + 512:2 * D], start=False, stop=True)
        peq = None
        if mine:
            p.PE("matmul", pq[:, 0:512], ones_sb[0:1, :], bqkv_sb[0:1, 0:512],
                 start=False, stop=True)
            peq = p.PE("matmul", pq[:, 512:1024], ones_sb[0:1, :],
                       bqkv_sb[0:1, 512:1024], start=False, stop=True)
        kdst = bfA if i % 2 == 0 else xstg2
        if i >= 2:
            p.wait("DV", "stk%d" % (i % 2), stk_tok[i - 2])
        d3k = rope(pk, kdst, pek)
        p.wait("AC", "DV", d3k)
        ack = p.AC("activation", sC[:, 0:D], kdst[:, 0:D], AF.Square)
        p.wait("SY", "DV", d3k)
        stk_tok[i] = p.dma("stk%d" % (i % 2), k_scr[i * 128:(i + 1) * 128, :],
                           kdst[:, 0:D])
        p.wait("DV", "AC", ack)
        dk = p.DV("tensor_reduce", sD[:, 0:16].unsqueeze(2),
                  sC[:, 0:D].rearrange("p (h d) -> p h d", d=HD), AX.X, ALU.add)
        p.wait("DV", "DV", dk)
        dkm = p.DV("tensor_max", nrm[:, 16:32], nrm[:, 16:32], sD[:, 0:16])
        p.wait("DV", "DV", dkm)
        if mine:
            qdst = bfB if i % 2 == 0 else xstg3
            if i >= 2:
                p.wait("DV", "stq%d" % (i % 2), stq_tok[i - 2])
            d3q = rope(pq, qdst, peq)
            p.wait("AC", "DV", d3q)
            acq = p.AC("activation", sC[:, 0:D], qdst[:, 0:D], AF.Square)
            p.wait("SY", "DV", d3q)
            stq_tok[i] = p.dma("stq%d" % (i % 2),
                               q_scr[i * 128:(i + 1) * 128, :], qdst[:, 0:D])
            p.wait("DV", "AC", acq)
            dq = p.DV("tensor_reduce", sD[:, 0:16].unsqueeze(2),
                      sC[:, 0:D].rearrange("p (h d) -> p h d", d=HD), AX.X, ALU.add)
            p.wait("DV", "DV", dq)
            dqm = p.DV("tensor_max", nrm[:, 0:16], nrm[:, 0:16], sD[:, 0:16])
            p.wait("DV", "DV", dqm)
        qk_dv[i] = p.cnt["DV"]
        dq_step(1)

    preduce(nrm[:, 0:32], 32, 16)
    dv = p.DV("tensor_add", scal[0:1, 48:64], scal[0:1, 16:32], scal[0:1, 32:48])
    p.wait("DV", "DV", dv)
    s2v = scal[0:1, 48:64].rearrange("o (h two) -> o h two", two=2)
    dv = p.DV("tensor_max", scal[0:1, 48:56], s2v[:, :, 0:1].squeeze(2),
              s2v[:, :, 1:2].squeeze(2))
    p.wait("DV", "DV", dv)
    dv = p.DV("tensor_scalar", scal[0:1, 56:64], scal[0:1, 48:56],
              -0.502, -0.01, ALU.mult, ALU.add)
    p.wait("DV", "DV", dv)
    bcast_scal(56, 8, scbc[:, 8:16], "bc2")

    maxc = stat[:, 6 * NTB:7 * NTB]
    tmpc = stat[:, 7 * NTB:8 * NTB]
    p.wait("DV", "stxn0")
    p.wait("DV", "stxn1")
    p.wait("DV", "DV")
    dv = p.DV("tensor_reduce", tmpc[:, 0:1].unsqueeze(2),
              maxc.unsqueeze(1), AX.X, ALU.max)
    preduce(tmpc[:, 0:1], 1, 0)
    dv = p.DV("tensor_scalar", scal[0:1, 1:2], scal[0:1, 0:1],
              1.0 / QMAXI, 1e-12, ALU.mult, ALU.max)
    p.wait("DV", "DV", dv)
    dv = p.DV("tensor_copy", bfC[0:1, 0:1], scal[0:1, 1:2])
    p.wait("DV", "DV", dv)
    dv = p.DV("tensor_copy", scal[0:1, 2:3], bfC[0:1, 0:1])
    p.wait("DV", "DV", dv)
    dv = p.DV("reciprocal", scal[0:1, 3:4], scal[0:1, 2:3])
    p.wait("DV", "DV", dv)
    bcast_scal(2, 2, scbc[:, 0:2], "bc1", dv_tok=dv)

    p.wait("DV", "bc1")
    p.wait("DV", "ldxnT")
    p.wait("DV", "stxn0")
    p.wait("DV", "stxn1")
    for _k in range(2):
        p.wait("DV", "stk%d" % _k)
        p.wait("DV", "stq%d" % _k)
    SL = min(1024, TOK)
    for c in range(NCD):
        for t0 in range(0, TOK, SL):
            d1 = p.DV("tensor_scalar", bfC[:, 0:SL], xnT[:, c, t0:t0 + SL],
                      scbc[:, 1:2], None, ALU.mult)
            p.wait("DV", "DV", d1)
            d2 = p.DV("tensor_scalar", sA[:, 0:SL], bfC[:, 0:SL],
                      C_RNE, None, ALU.add)
            p.wait("DV", "DV", d2)
            d3 = p.DV("tensor_scalar", x_q[:, c, t0:t0 + SL], sA[:, 0:SL],
                      C_RNE, scbc[:, 0:1], ALU.subtract, ALU.mult)
            p.wait("DV", "DV", d3)

    st_tok = {}
    fidx = [0]
    IH = I // 2
    pus = [P2[:, 0:512], P2[:, 512:1024], P34[:, 0:512], P34[:, 512:1024]]
    ebs = [sA, sB, sC, sD]

    def ffn_T(dq_scr, stream, h, pe_tok):
        p.wait("SY", "stwq" + stream)
        p.wait("SY", "PE", pe_tok)
        for c in range(NCD):
            p.dmat("ldwt" + stream + str(h),
                   wdqT[:, c, h * IH:(h + 1) * IH],
                   dq_scr[h * IH:(h + 1) * IH, c * 128:(c + 1) * 128])

    def ffn_half(out_scr, maxslot, stream, h):
        p.wait("PE", "ldwt" + stream + str(h))
        dvz = p.DV("memset", nrm[:, maxslot:maxslot + 1], 0.0) \
            if h == 0 else None
        if dvz is not None:
            p.wait("DV", "DV", dvz)
        pe = None
        for jj in range(h * (NCI // 2), (h + 1) * (NCI // 2)):
            for g in range(NTG):
                idx = fidx[0]
                fidx[0] += 1
                pu = pus[idx % 4]
                if idx >= 4:
                    p.wait("PE", "DV", st_tok[idx - 4][0])
                for c in range(NCD):
                    pe = p.PE("matmul", pu[:, 0:GSZ],
                              wdqT[:, c, jj * 128:(jj + 1) * 128],
                              x_q[:, c, g * GSZ:(g + 1) * GSZ],
                              start=(c == 0), stop=(c == NCD - 1))
                p.wait("DV", "PE", pe)
                eb = ebs[idx % 4]
                if idx >= 4:
                    p.wait("DV", st_tok[idx - 4][2], st_tok[idx - 4][1])
                d1 = p.DV("tensor_copy", eb[:, 0:GSZ], pu[:, 0:GSZ])
                p.DV("tensor_reduce", nrm[:, 42:43].unsqueeze(2),
                     eb[:, 0:GSZ].unsqueeze(1), AX.X, ALU.max,
                     apply_absolute_value=True)
                dacc = p.DV("tensor_max", nrm[:, maxslot:maxslot + 1],
                            nrm[:, maxslot:maxslot + 1], nrm[:, 42:43])
                p.wait("DV", "DV", dacc)
                p.wait("SY", "DV", d1)
                st = p.dma(stream + "o%d" % (idx % 4),
                           out_scr[jj * 128:(jj + 1) * 128,
                                   g * GSZ:(g + 1) * GSZ],
                           eb[:, 0:GSZ])
                st_tok[idx] = (p.cnt["DV"], st, stream + "o%d" % (idx % 4))
        return pe

    dq_finish()
    p.wait("SY", "PE")
    ffn_T(wdq_scr, "ldwu", 0, None)
    ffn_T(wdq_scr, "ldwu", 1, None)
    p.wait("PE", "DV")
    pe0 = ffn_half(up_scr, 32, "ldwu", 0)
    ffn_T(wdqg_scr, "ldwg", 0, pe0)
    pe1 = ffn_half(up_scr, 32, "ldwu", 1)
    ffn_T(wdqg_scr, "ldwg", 1, pe1)
    ffn_half(gate_scr, 33, "ldwg", 0)
    ffn_half(gate_scr, 33, "ldwg", 1)
    for _k in range(4):
        p.wait("SY", "ldwuo%d" % _k)
        p.wait("SY", "ldwgo%d" % _k)
    preduce(nrm[:, 32:34], 2, 4)
    dv = p.DV("tensor_scalar", scal[0:1, 6:8], scal[0:1, 4:6],
              1.0 / QMAXI, 1e-12, ALU.mult, ALU.max)
    p.wait("DV", "DV", dv)
    dv = p.DV("reciprocal", scal[0:1, 8:10], scal[0:1, 6:8])
    p.wait("DV", "DV", dv)
    bcast_scal(6, 4, scbc[:, 2:6], "bc3")

    PWN = (I * TOK) // (128 * PW)
    NGW = TOK // PW
    import heapq
    sched = []
    scnt = [0]
    seqc = [0]

    def schedule(delay, fn):
        seqc[0] += 1
        heapq.heappush(sched, (scnt[0] + delay, seqc[0], fn))

    def slot():
        scnt[0] += 1
        if sched and sched[0][0] <= scnt[0]:
            heapq.heappop(sched)[2]()

    def drain():
        while sched:
            slot()

    pa_ld, pa_q2, pa_a1, pa_d5, pa_st = {}, {}, {}, {}, {}
    pa_acc = [None]
    pa2_done = [0]
    pb_l1, pb_l2, pb_d5, pb_st, pb_red = {}, {}, {}, {}, {}
    pb_acc = [None]
    pb2_done = [0]

    def pa_load(t):
        gb = ph5_in[t % 4]
        jj, g = t // NGW, t % NGW
        if t < 4:
            for _k in range(4):
                p.wait("SY", "ldwgo%d" % _k)
        else:
            p.wait("SY", "DV", pa_d5[t - 4])
        pa_ld[t] = p.dma("ldg%d" % (t % 4), gb,
                         gate_scr[jj * 128:(jj + 1) * 128, g * PW:(g + 1) * PW])
        schedule(6, lambda: pa_quant(t))

    def pa_quant(t):
        gb = ph5_in[t % 4]
        tmp = sC if t % 2 == 0 else sB
        jj, g = t // NGW, t % NGW
        if t >= 2 and (t - 2) not in pa_d5:
            schedule(1, lambda: pa_quant(t))
            return
        if t < 4:
            p.wait("DV", "bc3")
        p.wait("DV", "ldg%d" % (t % 4), pa_ld[t])
        p.DV("tensor_scalar", tmp[:, 0:PW], gb, scbc[:, 5:6], C_RNE,
             ALU.mult, ALU.add)
        pa_q2[t] = p.DV("tensor_scalar", gb, tmp[:, 0:PW], C_RNE, scbc[:, 3:4],
                        ALU.subtract, ALU.mult)
        schedule(3, lambda: pa_act(t))

    def pa_act(t):
        gb = ph5_in[t % 4]
        tmp = sC if t % 2 == 0 else sB
        p.wait("AC", "DV", pa_q2[t])
        pa_a1[t] = p.AC("activation", tmp[:, 0:PW], gb, AF.Tanh, scale=0.5)
        schedule(2, lambda: pa_fin(t))

    def pa_fin(t):
        gb = ph5_in[t % 4]
        ob = ph5_out[t % 4]
        tmp = sC if t % 2 == 0 else sB
        jj, g = t // NGW, t % NGW
        p.wait("DV", "AC", pa_a1[t])
        p.DV("tensor_scalar", tmp[:, 0:PW], tmp[:, 0:PW],
             32768.0, 32768.0 + C_RNE, ALU.mult, ALU.add)
        p.DV("tensor_scalar", tmp[:, 0:PW], tmp[:, 0:PW],
             C_RNE, 65535.0, ALU.subtract, ALU.min)
        if t >= 4:
            p.wait("DV", "stt1%d" % (t % 4), pa_st[t - 4])
        d5 = p.DV("scalar_tensor_tensor", ob, tmp[:, 0:PW],
                  float(2.0 ** -16), gb, ALU.mult, ALU.mult)
        p.DV("tensor_reduce", sF[:, 0:1].unsqueeze(2), ob.unsqueeze(1),
             AX.X, ALU.max, apply_absolute_value=True)
        pa_acc[0] = p.DV("tensor_max", nrm[:, 34:35], nrm[:, 34:35],
                         sF[:, 0:1])
        pa_d5[t] = d5
        p.wait("SY", "DV", d5)
        pa_st[t] = p.dma("stt1%d" % (t % 4),
                         t1_scr[jj * 128:(jj + 1) * 128, g * PW:(g + 1) * PW],
                         ob)
        pa2_done[0] += 1
        if t + 4 < PWN:
            schedule(29, lambda: pa_load(t + 4))
        if pa2_done[0] == PWN:
            schedule(2, m4_resolve)

    def m4_resolve():
        preduce(nrm[:, 34:35], 1, 10, dv_tok=pa_acc[0])
        dv = p.DV("tensor_scalar", scal[0:1, 11:12], scal[0:1, 10:11],
                  1.0 / QMAXI, 1e-12, ALU.mult, ALU.max)
        p.wait("DV", "DV", dv)
        dv = p.DV("reciprocal", scal[0:1, 12:13], scal[0:1, 11:12])
        p.wait("DV", "DV", dv)
        bcast_scal(11, 2, scbc[:, 6:8], "bc4", dv_tok=dv)
        p.DV("memset", nrm[:, 36:37], 0.0)
        for _t in range(min(4, PWN)):
            schedule(2 + 3 * _t, (lambda tt: lambda: pb_load(tt))(_t))

    def pb_load(t):
        tb = ph5_t1[t % 4]
        ub = ph5_up[t % 4]
        jj, g = t // NGW, t % NGW
        if t < 4:
            for _k in range(4):
                p.wait("SY", "ldwuo%d" % _k)
            p.wait("SY", "stt1%d" % (t % 4))
            p.wait("SY", "DV", pa_acc[0])
        else:
            p.wait("SY", "DV", pb_red[t - 4])
            p.wait("SY", "stt2%d" % (t % 4), pb_st[t - 4])
            p.wait("SY", "stt1%d" % (t % 4))
        pb_l1[t] = p.dma("ldt1%d" % (t % 4), tb,
                         t1_scr[jj * 128:(jj + 1) * 128, g * PW:(g + 1) * PW])
        pb_l2[t] = p.dma("ldu%d" % (t % 4), ub,
                         up_scr[jj * 128:(jj + 1) * 128, g * PW:(g + 1) * PW])
        schedule(9, lambda: pb_quant(t))

    def pb_quant(t):
        tb = ph5_t1[t % 4]
        ub = ph5_up[t % 4]
        jj, g = t // NGW, t % NGW
        if t < 4:
            p.wait("DV", "bc4")
        p.wait("DV", "ldt1%d" % (t % 4), pb_l1[t])
        p.wait("DV", "ldu%d" % (t % 4), pb_l2[t])
        p.DV("tensor_scalar", tb, tb, scbc[:, 7:8], C_RNE, ALU.mult, ALU.add)
        p.DV("tensor_scalar", tb, tb, C_RNE, scbc[:, 6:7],
             ALU.subtract, ALU.mult)
        p.DV("tensor_scalar", ub, ub, scbc[:, 4:5], C_RNE, ALU.mult, ALU.add)
        p.DV("tensor_scalar", ub, ub, C_RNE, scbc[:, 2:3],
             ALU.subtract, ALU.mult)
        d5 = p.DV("tensor_mul", tb, tb, ub)
        pb_red[t] = p.DV("tensor_reduce", sF[:, 0:1].unsqueeze(2),
                         tb.unsqueeze(1), AX.X, ALU.max,
                         apply_absolute_value=True)
        pb_acc[0] = p.DV("tensor_max", nrm[:, 36:37], nrm[:, 36:37],
                         sF[:, 0:1])
        pb_d5[t] = d5
        p.wait("SY", "DV", d5)
        for sub in range(PW // 128):
            tt = (g * PW) // 128 + sub
            p.dma("stt2%d" % (t % 4), t2_scr[tt, jj * 128:(jj + 1) * 128, :],
                  tb[:, sub * 128:(sub + 1) * 128])
        pb_st[t] = p.cnt["stt2%d" % (t % 4)]
        pb2_done[0] += 1
        if t + 4 < PWN:
            schedule(25, lambda: pb_load(t + 4))
        if pb2_done[0] == PWN:
            schedule(2, m5_resolve)

    def m5_resolve():
        preduce(nrm[:, 36:37], 1, 13, dv_tok=pb_acc[0])
        dv = p.DV("tensor_scalar", scal[0:1, 14:15], scal[0:1, 13:14],
                  1.0 / QMAXI, 1e-12, ALU.mult, ALU.max)
        p.wait("DV", "DV", dv)
        dv = p.DV("reciprocal", scal[0:1, 15:16], scal[0:1, 14:15])
        p.wait("DV", "DV", dv)
        bcast_scal(14, 2, nrm[:, 40:42], "bc5", dv_tok=dv)

    p.DV("memset", nrm[:, 34:35], 0.0)
    for _t in range(min(4, PWN)):
        schedule(1 + 3 * _t, (lambda tt: lambda: pa_load(tt))(_t))

    p.wait("SY", "PE")
    for c in range(NCD):
        p.dma("ldwo", woT_sb[:, c, :], woT[c * 128:(c + 1) * 128, :])
    for _k in range(2):
        p.wait("SY", "stk%d" % _k)
        p.wait("SY", "stq%d" % _k)
    p.wait("SY", "stv0")
    p.wait("SY", "stv1")

    def load_pair(pp, buf):
        b = ktq[buf]
        vb = vpair[buf]
        sn = "ldpair%d" % buf
        p.dmat(sn, b[:, 0:S], k_scr[:, pp * 128:(pp + 1) * 128])
        p.dmat(sn, b[:, S:S + TOK], q_scr[:, pp * 128:(pp + 1) * 128])
        for g in range(2):
            p.dma(sn, vb[:, :, g * 65:g * 65 + 64],
                  v_scr[:, pp * 128 + g * 64:pp * 128 + (g + 1) * 64]
                  .rearrange("(c p) e -> p c e", p=128))
        return p.cnt[sn]

    n2_pe, n3_dv, n4_pl, n5_pe, n6_dv, n1_dv = {}, {}, {}, {}, {}, {}

    def norm_chain(pp_, qb_, k_, ci, ev_tok):
        slab = oring[k_]
        rb = (ci % 2) * 2 * QBS
        r0 = 64 if ci % 2 == 0 else 32
        rc = [None]

        def n1():
            if ci >= 2 and (ci - 2) not in n2_pe:
                schedule(1, n1)
                return
            if ci >= 2:
                p.wait("DV", "PE", n2_pe[ci - 2])
            p.wait("DV", "AC", ev_tok)
            rc[0] = p.DV("reciprocal", rr[r0:r0 + 1, 0:2 * QBS],
                         nsc[64:65, rb:rb + 2 * QBS])
            n1_dv[ci] = rc[0]
            schedule(3, n2)

        def n2():
            if ci >= 1 and (ci - 1) not in n3_dv:
                schedule(1, n2)
                return
            p.wait("PE", "DV", rc[0])
            if ci >= 1:
                p.wait("PE", "DV", n3_dv[ci - 1])
            p.PE("matmul", P34[0:64, 0:QBS], onesP64[r0:r0 + 1, 0:64],
                 rr[r0:r0 + 1, 0:QBS], start=True, stop=True)
            n2_pe[ci] = p.PE("matmul", P34[0:64, QBS:2 * QBS],
                             onesP64[r0:r0 + 1, 0:64],
                             rr[r0:r0 + 1, QBS:2 * QBS], start=True, stop=True)
            schedule(1, n3)

        def n3():
            if ci >= 2 and (ci - 2) not in n4_pl:
                schedule(1, n3)
                return
            p.wait("DV", "PE", n2_pe[ci])
            if ci >= 2:
                p.wait("DV", "PL", n4_pl[ci - 2])
            n3_dv[ci] = p.DV("tensor_copy", nsc[0:64, rb:rb + 2 * QBS],
                             P34[0:64, 0:2 * QBS])
            schedule(1, n4)

        def n4():
            if ci >= 1 and (ci - 1) not in n5_pe:
                schedule(1, n4)
                return
            p.wait("PL", "DV", n3_dv[ci])
            p.wait("PL", "AC", ev_tok)
            if ci >= 1:
                p.wait("PL", "PE", n5_pe[ci - 1])
            p.PL("tensor_tensor", o_attn[0:64, pp_, qb_ * QBS:(qb_ + 1) * QBS],
                 slab[0:64, 0:QBS], nsc[0:64, rb:rb + QBS], ALU.mult)
            n4_pl[ci] = p.PL("tensor_tensor", bfB[0:64, 0:QBS],
                             slab[0:64, QBS:2 * QBS],
                             nsc[0:64, rb + QBS:rb + 2 * QBS], ALU.mult)
            schedule(1, n5)

        def n5():
            if ci >= 1 and (ci - 1) not in n6_dv:
                schedule(1, n5)
                return
            p.wait("PE", "PL", n4_pl[ci])
            if ci >= 1:
                p.wait("PE", "DV", n6_dv[ci - 1])
            n5_pe[ci] = p.PE("matmul", P34[64:128, 0:QBS], id64_sb[:, :],
                             bfB[0:64, 0:QBS], start=True, stop=True,
                             tile_position=(0, 64))
            schedule(1, n6)

        def n6():
            p.wait("DV", "PE", n5_pe[ci])
            n6_dv[ci] = p.DV("tensor_copy",
                             o_attn[64:128, pp_, qb_ * QBS:(qb_ + 1) * QBS],
                             P34[64:128, 0:QBS])

        return n1

    rq = [0]
    p.wait("DV", "PL")
    ld_tok = {0: load_pair(0, 0)}
    pe_after_pair = {}
    for pp in range(PAIRS):
        buf = pp % 2
        p.wait("DV", "ldpair%d" % buf, ld_tok[pp])
        dvo = p.DV("memset",
                   vpair[buf].rearrange("p c (g e) -> p c g e", g=2, e=65)
                   [:, :, :, 64:65], 1.0)
        p.wait("PE", "DV", dvo)
        p.wait("PE", "ldpair%d" % buf, ld_tok[pp])
        if pp + 1 < PAIRS:
            if pp >= 1:
                p.wait("SY", "PE", pe_after_pair[pp - 1])
            ld_tok[pp + 1] = load_pair(pp + 1, (pp + 1) % 2)
        b = ktq[buf]
        vb = vpair[buf]
        p.wait("AC", "bc2")
        for qb in range(NQB):
            exp_tok = {}
            pv_tok = {}
            sc_tok = {}

            def emit_scores(j):
                sc = P0 if j % 2 == 0 else P1
                if j >= 2:
                    p.wait("PE", "AC", exp_tok[j - 2])
                p.PE("matmul", sc[:, 0:QBS], b[0:64, j * 128:(j + 1) * 128],
                     b[0:64, S + qb * QBS:S + (qb + 1) * QBS],
                     start=True, stop=True, tile_position=(0, 0))
                sc_tok[j] = p.PE("matmul", sc[:, QBS:2 * QBS],
                                 b[64:128, j * 128:(j + 1) * 128],
                                 b[64:128, S + qb * QBS:S + (qb + 1) * QBS],
                                 start=True, stop=True, tile_position=(64, 0))

            emit_scores(0)
            for j in range(KCH):
                sc = P0 if j % 2 == 0 else P1
                ptb = pt[j % 2]
                p.wait("AC", "PE", sc_tok[j])
                if j >= 2:
                    p.wait("AC", "PE", pv_tok[j - 2])
                ace = p.AC("activation", ptb[:, 0:2 * QBS], sc[:, 0:2 * QBS],
                           AF.Exp, bias=scbc[:, 8 + pp:9 + pp], scale=1.0)
                exp_tok[j] = ace
                if j + 1 < KCH:
                    emit_scores(j + 1)
                p.wait("PE", "AC", ace)
                p.PE("matmul", P2[0:65, 0:QBS], vb[:, j, 0:65], ptb[:, 0:QBS],
                     start=(j == 0), stop=(j == KCH - 1))
                pev = p.PE("matmul", P2[0:65, 512:512 + QBS], vb[:, j, 65:130],
                           ptb[:, QBS:2 * QBS],
                           start=(j == 0), stop=(j == KCH - 1))
                pv_tok[j] = pev
                slot()
            ci = rq[0]
            k_ = ci % 4
            rq[0] += 1
            rb = (ci % 2) * 2 * QBS
            while ci >= 4 and (ci - 4) not in n4_pl:
                slot()
            while ci >= 2 and (ci - 2) not in n1_dv:
                slot()
            p.wait("DV", "PE", pv_tok[KCH - 1])
            p.DV("tensor_copy", nsc[64:65, rb:rb + QBS], P2[64:65, 0:QBS])
            devd = p.DV("tensor_copy", nsc[64:65, rb + QBS:rb + 2 * QBS],
                        P2[64:65, 512:512 + QBS])
            p.wait("AC", "PE", pv_tok[KCH - 1])
            if ci >= 4:
                p.wait("AC", "PL", n4_pl[ci - 4])
            p.AC("activation", oring[k_][0:64, 0:QBS], P2[0:64, 0:QBS],
                 AF.Copy)
            dev = p.AC("activation", oring[k_][0:64, QBS:2 * QBS],
                       P2[0:64, 512:512 + QBS], AF.Copy)
            schedule(2, norm_chain(pp, qb, k_, ci, dev))
            p.wait("PE", "AC", dev)
            p.wait("PE", "DV", devd)
        pe_after_pair[pp] = p.cnt["PE"]

    drain()
    p.wait("PE", "DV")
    p.wait("PE", "PL")
    if S <= 512:
        odbg = nc.dram_tensor("o_dbg", [NCD * 128, TOK], BF16, kind="Internal").ap()
        p.wait("SY", "DV")
        p.wait("SY", "PL")
        for c in range(NCD):
            p.dma("odbg", odbg[c * 128:(c + 1) * 128, :], o_attn[:, c, :])
        p.wait("SY", "odbg")
    p.wait("SY", "PE")
    p.wait("SY", "DV")
    p.wait("SY", "PL")
    p.wait("SY", "stwqldwd")
    for _k in range(4):
        p.wait("SY", "stt2%d" % _k)
        p.wait("SY", "ldt1%d" % _k)
        p.wait("SY", "ldu%d" % _k)
        p.wait("SY", "ldg%d" % _k)
        p.wait("SY", "stt1%d" % _k)
    t2_ld = {}
    NSP = NCI // 4

    def ldt2_issue(i):
        t2b = t2sl[i % 3]
        src = t2_scr[i].rearrange("(c p) t -> p c t", p=128)
        t2_ld[i] = [p.dma("ldt2%d_%d" % (i % 3, k),
                          t2b[:, k * NSP:(k + 1) * NSP, :],
                          src[:, k * NSP:(k + 1) * NSP, :], q="PL")
                    for k in range(4)]

    p.wait("PL", "PE")
    p.wait("PL", "DV")
    for _k in range(4):
        p.wait("PL", "stt2%d" % _k)
        p.wait("PL", "ldt1%d" % _k)
        p.wait("PL", "ldu%d" % _k)
        p.wait("PL", "ldg%d" % _k)
        p.wait("PL", "stt1%d" % _k)
    for i in range(min(3, NTM)):
        ldt2_issue(i)
    p.wait("AC", "PE")
    p.wait("AC", "stwqldwd")
    for c in range(NCI):
        for h in range(2):
            p.dmat("ldwdt%d" % ((2 * c + h) % 6),
                   wddqT[:, c, h * (D // 2):(h + 1) * (D // 2)],
                   wddq_scr[h * (D // 2):(h + 1) * (D // 2),
                            c * 128:(c + 1) * 128],
                   q="SY")

    woev = [carve(o1 + NCI * D * 2 + NCI * 128 * 4 + k * 2048, 2048, BF16,
                  [128, 1024]) for k in range(2)]
    p.wait("PE", "ldwo")
    p.wait("PE", "AC")
    p.wait("PE", "DV")
    wo_ev = {}
    fin = {}
    fin_ld = {}
    fin_d2 = {}

    def wo_chunk(i):
        pw = P0 if i % 2 == 0 else P1
        if i >= 2:
            p.wait("PE", "DV", wo_ev[i - 2][0])
        for c in range(NCD):
            p.PE("matmul", pw[:, 0:512], o_attn[:, c, i * 128:(i + 1) * 128],
                 woT_sb[:, c, 0:512], start=(c == 0), stop=False)
            p.PE("matmul", pw[:, 512:1024], o_attn[:, c, i * 128:(i + 1) * 128],
                 woT_sb[:, c, 512:1024], start=(c == 0), stop=False)
        p.PE("matmul", pw[:, 0:512], ones_sb[0:1, :], bo_sb[0:1, 0:512],
             start=False, stop=True)
        pe = p.PE("matmul", pw[:, 512:1024], ones_sb[0:1, :],
                  bo_sb[0:1, 512:1024], start=False, stop=True)
        p.wait("DV", "PE", pe)
        eb = woev[i % 2]
        if i >= 2:
            p.wait("DV", "sts%d" % (i % 2), wo_ev[i - 2][1])
        dv = p.DV("tensor_copy", eb, pw[:, 0:1024])
        p.wait("SY", "DV", dv)
        st = p.dma("sts%d" % (i % 2), s_scr[i * 128:(i + 1) * 128, :], eb)
        wo_ev[i] = (dv, st)

    def fin_q(i):
        par = i % 2
        t2b, ob = t2sl[i % 3], osl[par]
        if i == 0:
            p.wait("DV", "bc5")
            p.wait("DV", "PL")
        if i >= 2:
            assert fin[i - 2][3] is not None
            p.wait("SY", "DV", fin[i - 2][3])
            p.wait("SY", "PE", fin[i - 2][0])
        tokx = p.dma("ldx3%d" % par, bfB[:, 0:D] if par == 0 else bfA[:, 0:D],
                     xb[i * 128:(i + 1) * 128, :])
        p.wait("SY", "sts%d" % par, wo_ev[i][1])
        toks = p.dma("lds3%d" % par,
                     bfC[:, 0:D] if par == 0 else rr[:, 0:512].bitcast(BF16),
                     s_scr[i * 128:(i + 1) * 128, :])
        fin_ld[i] = (tokx, toks)
        for k in range(4):
            p.wait("DV", "ldt2%d_%d" % (i % 3, k), t2_ld[i][k])
        if i >= 2:
            p.wait("DV", "PE", fin[i - 2][0])
        p.DV("tensor_scalar", t2b.rearrange("p a b -> p (a b)"),
             t2b.rearrange("p a b -> p (a b)"),
             nrm[:, 41:42], C_RNE, ALU.mult, ALU.add)
        d2 = p.DV("tensor_scalar", ob.rearrange("p a b -> p (a b)"),
                  t2b.rearrange("p a b -> p (a b)"),
                  C_RNE, nrm[:, 40:41], ALU.subtract, ALU.mult)
        fin_d2[i] = d2
        if i + 3 < NTM:
            p.wait("PL", "DV", d2)
            ldt2_issue(i + 3)

    def fin_m(i):
        par = i % 2
        ob = osl[par]
        pd = P2 if par == 0 else P34
        if i == 0:
            for _k in range(6):
                p.wait("PE", "ldwdt%d" % _k)
        p.wait("PE", "DV", fin_d2[i])
        if i >= 2:
            p.wait("PE", "AC", fin[i - 2][1])
        pe = None
        for c in range(NCI):
            p.PE("matmul", pd[:, 0:512], ob[:, c, :], wddqT[:, c, 0:512],
                 start=(c == 0), stop=(c == NCI - 1))
            pe = p.PE("matmul", pd[:, 512:1024], ob[:, c, :],
                      wddqT[:, c, 512:1024],
                      start=(c == 0), stop=(c == NCI - 1))
        p.wait("AC", "PE", pe)
        ga = p.AC("activation", sA[:, 0:D] if par == 0 else sB[:, 0:D],
                  pd[:, 0:1024], AF.Silu)
        fin[i] = (pe, ga, None, None)

    def fin_combine(i):
        par = i % 2
        tokx, toks = fin_ld[i]
        ga = fin[i][1]
        p.wait("DV", "AC", ga)
        p.wait("DV", "ldx3%d" % par, tokx)
        if i >= 2:
            p.wait("DV", "sty%d" % par, fin[i - 2][2])
        p.wait("DV", "lds3%d" % par, toks)
        sb_t = bfC[:, 0:D] if par == 0 else rr[:, 0:512].bitcast(BF16)
        ga_t = sA[:, 0:D] if par == 0 else sB[:, 0:D]
        xb_t = bfB[:, 0:D] if par == 0 else bfA[:, 0:D]
        dcmb = sD if par == 0 else sE
        p.DV("scalar_tensor_tensor", dcmb[:, 0:512],
             ga_t[:, 0:512], 1.0, sb_t[:, 0:512], ALU.mult, ALU.mult)
        cm2 = rr[:, 512:1024] if par == 0 else sC[:, 0:512]
        p.DV("scalar_tensor_tensor", cm2,
             ga_t[:, 512:1024], 1.0, sb_t[:, 512:1024], ALU.mult, ALU.mult)
        p.DV("tensor_add", dcmb[:, 0:512], dcmb[:, 0:512], xb_t[:, 0:512])
        d4b = p.DV("tensor_add", cm2, cm2, xb_t[:, 512:1024])
        p.wait("SY", "DV", d4b)
        p.dma("sty%d" % par, y[i * 128:(i + 1) * 128, 0:512], dcmb[:, 0:512])
        st = p.dma("sty%d" % par, y[i * 128:(i + 1) * 128, 512:1024], cm2)
        fin[i] = (fin[i][0], fin[i][1], st, d4b)

    for it in range(NTM + 4):
        if it < NTM:
            wo_chunk(it)
        if 0 <= it - 4 < NTM:
            fin_combine(it - 4)
        if 0 <= it - 2 < NTM:
            fin_q(it - 2)
        if 0 <= it - 3 < NTM:
            fin_m(it - 3)
    for s in list(p.cnt.keys()):
        p.wait("SY", s)
    return nc


def _prep_inputs(cfg, inp):
    S, D, HD = cfg["S"], cfg["D"], cfg["HD"]
    TOK = cfg["TOK"]
    x = np.asarray(inp["L0_attn_input_embeds"])
    Wq = np.asarray(inp["L0_attn_Wq"]); bq = np.asarray(inp["L0_attn_bq"])
    Wk = np.asarray(inp["L0_attn_Wk"]); bk = np.asarray(inp["L0_attn_bk"])
    Wv = np.asarray(inp["L0_attn_Wv"]); bv = np.asarray(inp["L0_attn_bv"])
    Wo = np.asarray(inp["L0_attn_Wo"]); bo = np.asarray(inp["L0_attn_bo"])
    wg = np.ascontiguousarray(np.asarray(inp["L0_ffn_w_gate"]))
    wu = np.ascontiguousarray(np.asarray(inp["L0_ffn_w_up"]))
    wd = np.ascontiguousarray(np.asarray(inp["L0_ffn_w_down"]))
    wqkvT = np.ascontiguousarray(np.concatenate([Wq.T, Wk.T, Wv.T], axis=1))
    woT = np.ascontiguousarray(Wo.T)
    bqkv = np.ascontiguousarray(np.concatenate([bq, bk, bv]).reshape(1, 3 * D))
    bo_r = np.ascontiguousarray(bo.reshape(1, D))
    inv = (1.0 / (10000.0 ** (np.arange(0, HD, 2, dtype=np.float32) / HD))).astype(np.float32)
    t = np.arange(S, dtype=np.float32)
    fr = np.einsum("s,d->sd", t, inv).astype(np.float32)
    emb = np.concatenate([fr, fr], axis=1)
    cosb = np.cos(emb).astype(BF).astype(np.float32)
    sinb = np.sin(emb).astype(BF).astype(np.float32)
    sinb[:, 0:HD // 2] *= -1.0
    cosb = np.ascontiguousarray(cosb * np.float32(INV_SQRT8))
    sinb = np.ascontiguousarray(sinb * np.float32(INV_SQRT8))
    in_maps = []
    for c in range(8):
        b, half = c // 2, c % 2
        xbat = x[b]
        if half == 1:
            xb_c = np.ascontiguousarray(np.concatenate([xbat[TOK:], xbat[:TOK]], axis=0))
            tc = np.ascontiguousarray(np.concatenate([cosb[TOK:], cosb[:TOK]], axis=0))
            ts_ = np.ascontiguousarray(np.concatenate([sinb[TOK:], sinb[:TOK]], axis=0))
        else:
            xb_c = np.ascontiguousarray(xbat)
            tc, ts_ = cosb, sinb
        in_maps.append({
            "xb": xb_c,
            "wqkvT": wqkvT, "woT": woT, "bqkv": bqkv, "bo": bo_r,
            "wg": wg, "wu": wu, "wd": wd, "tabc": tc, "tabs": ts_,
        })
    return in_maps


def kernel(**inputs):
    cfg = mkcfg()
    in_maps = _prep_inputs(cfg, inputs)
    nc = build_core(cfg)
    res = run_bass_kernel_spmd(nc, in_maps, core_ids=list(range(8)))
    B, S, D, TOK = cfg["B"], cfg["S"], cfg["D"], cfg["TOK"]
    out = np.zeros((B, S, D), np.float32)
    for c in range(8):
        b, half = c // 2, c % 2
        out[b, half * TOK:(half + 1) * TOK] = res.results[c]["y"]
    return out


# version 42
# speedup vs baseline: 1.0348x; 1.0348x over previous
import math

import numpy as np
import ml_dtypes

import concourse.bass as bass
import concourse.mybir as mybir
from concourse.bass_utils import run_bass_kernel_spmd

F32 = mybir.dt.float32
BF16 = mybir.dt.bfloat16
ALU = mybir.AluOpType
AF = mybir.ActivationFunctionType
AX = mybir.AxisListType
BF = ml_dtypes.bfloat16

C_RNE = float(np.float32(1.5 * 2.0 ** 23))
INV_SQRT8 = float(np.float32(1.0 / math.sqrt(8.0)))
QMAXI = 32767.0
EPS_LN = 1e-5


def mkcfg(S=4096):
    c = {}
    c["S"] = S
    c["B"] = 4
    c["D"] = 1024
    c["H"] = 16
    c["HD"] = 64
    c["I"] = 3072
    c["NTB"] = S // 128
    c["NTM"] = c["NTB"] // 2
    c["TOK"] = S // 2
    c["NCD"] = c["D"] // 128
    c["NCI"] = c["I"] // 128
    c["PAIRS"] = c["H"] // 2
    c["QBS"] = min(512, c["TOK"])
    c["NQB"] = c["TOK"] // c["QBS"]
    c["KCH"] = S // 128
    c["GSZ"] = min(512, c["TOK"])
    c["NTG"] = c["TOK"] // c["GSZ"]
    return c


class Prog:
    def __init__(self, nc):
        self.nc = nc
        self.eng = {"PE": nc.tensor, "DV": nc.vector, "AC": nc.scalar,
                    "PL": nc.gpsimd, "SY": nc.sync}
        self.sems = {}
        self.cnt = {}
        self.lastw = {}

    def _sem(self, name):
        if name not in self.sems:
            self.sems[name] = self.nc.alloc_semaphore(name)
            self.cnt[name] = 0
        return self.sems[name]

    def op(self, engname, inst):
        s = self._sem(engname)
        if engname in ("DV", "AC", "PL") and self.cnt[engname] > 0:
            inst._wait_ge(s, self.cnt[engname])
        inst.then_inc(s, 1)
        self.cnt[engname] += 1
        return self.cnt[engname]

    def PE(self, m, *a, **k):
        return self.op("PE", getattr(self.nc.tensor, m)(*a, **k))

    def DV(self, m, *a, **k):
        return self.op("DV", getattr(self.nc.vector, m)(*a, **k))

    def AC(self, m, *a, **k):
        return self.op("AC", getattr(self.nc.scalar, m)(*a, **k))

    def PL(self, m, *a, **k):
        return self.op("PL", getattr(self.nc.gpsimd, m)(*a, **k))

    def dma(self, stream, dst, src, q="SY", **kw):
        s = self._sem(stream)
        self.eng[q].dma_start(dst, src, **kw).then_inc(s, 16)
        self.cnt[stream] += 16
        return self.cnt[stream]

    def dmat(self, stream, dst, src, q="SY"):
        s = self._sem(stream)
        self.eng[q].dma_start(dst, src, transpose=True).then_inc(s, 16)
        self.cnt[stream] += 16
        return self.cnt[stream]

    def wait(self, engname, dep, val=None):
        if dep not in self.cnt:
            return
        val = self.cnt[dep] if val is None else val
        if val <= 0:
            return
        key = (engname, dep)
        if self.lastw.get(key, -1) >= val:
            return
        self.eng[engname].wait_ge(self._sem(dep), val)
        self.lastw[key] = val


def build_core(cfg):
    S, B, D, H, HD, I = cfg["S"], cfg["B"], cfg["D"], cfg["H"], cfg["HD"], cfg["I"]
    NTB, NTM, TOK = cfg["NTB"], cfg["NTM"], cfg["TOK"]
    NCD, NCI, PAIRS = cfg["NCD"], cfg["NCI"], cfg["PAIRS"]
    QBS, NQB, KCH = cfg["QBS"], cfg["NQB"], cfg["KCH"]
    GSZ, NTG = cfg["GSZ"], cfg["NTG"]

    nc = bass.Bass("TRN2", target_bir_lowering=False, num_devices=8)
    p = Prog(nc)

    xb = nc.dram_tensor("xb", [S, D], BF16, kind="ExternalInput").ap()
    wqkvT = nc.dram_tensor("wqkvT", [D, 3 * D], BF16, kind="ExternalInput").ap()
    woT = nc.dram_tensor("woT", [D, D], BF16, kind="ExternalInput").ap()
    bqkv = nc.dram_tensor("bqkv", [1, 3 * D], BF16, kind="ExternalInput").ap()
    bo = nc.dram_tensor("bo", [1, D], BF16, kind="ExternalInput").ap()
    wg_n = nc.dram_tensor("wg", [I, D], F32, kind="ExternalInput").ap()
    wu_n = nc.dram_tensor("wu", [I, D], F32, kind="ExternalInput").ap()
    wd_n = nc.dram_tensor("wd", [D, I], F32, kind="ExternalInput").ap()
    tabc = nc.dram_tensor("tabc", [S, HD], F32, kind="ExternalInput").ap()
    tabs = nc.dram_tensor("tabs", [S, HD], F32, kind="ExternalInput").ap()
    y = nc.dram_tensor("y", [TOK, D], F32, kind="ExternalOutput").ap()

    dscr = lambda nm, shp, dt: nc.dram_tensor(nm, shp, dt, kind="Internal").ap()
    xn_scr = dscr("xn_scr", [S, D], BF16)
    q_scr = dscr("q_scr", [TOK, D], BF16)
    k_scr = dscr("k_scr", [S, D], BF16)
    v_scr = dscr("v_scr", [S, D], BF16)
    wdq_scr = dscr("wdq_scr", [I, D], BF16)
    wdqg_scr = dscr("wdqg_scr", [I, D], BF16)
    wddq_scr = dscr("wddq_scr", [D, I], BF16)
    up_scr = dscr("up_scr", [I, TOK], F32)
    gate_scr = dscr("gate_scr", [I, TOK], F32)
    t1_scr = dscr("t1_scr", [I, TOK], F32)
    t2_scr = dscr("t2_scr", [NTM, I, 128], F32)
    s_scr = dscr("s_scr", [TOK, D], BF16)
    red_scr = dscr("red_scr", [64, 128], F32)

    ones_row = nc.inline_tensor(np.ones((1, 128), BF), "c_ones_row").ap()
    id64c = nc.inline_tensor(np.eye(64, dtype=BF), "c_id64").ap()
    onesP64c = nc.inline_tensor(np.ones((128, 64), np.float32), "c_onesP64").ap()
    onesB64c = nc.inline_tensor(np.ones((128, 64), BF), "c_onesB64").ap()

    sb = lambda nm, shp, dt: nc.alloc_sbuf_tensor(nm, shp, dt).ap()
    ones_sb = sb("ones_sb", [1, 128], BF16)
    id64_sb = sb("id64_sb", [64, 64], BF16)
    onesP64 = sb("onesP64", [128, 64], F32)
    onesB64 = sb("onesB64", [128, 64], BF16)
    bqkv_sb = sb("bqkv_sb", [1, 3 * D], BF16)
    bo_sb = sb("bo_sb", [1, D], BF16)
    stat = sb("stat", [128, 8 * NTB], F32)
    nrm = sb("nrm", [128, 48], F32)
    scal = sb("scal", [1, 64], F32)
    scbc = sb("scbc", [128, 16], F32)
    sA = sb("sA", [128, 1024], F32)
    sB = sb("sB", [128, 1024], F32)
    sC = sb("sC", [128, 1024], F32)
    sD = sb("sD", [128, 512], F32)
    sE = sb("sE", [128, 512], F32)
    sF = sb("sF", [128, 64], F32)
    bfA = sb("bfA", [128, 1024], BF16)
    bfB = sb("bfB", [128, 1024], BF16)
    bfC = sb("bfC", [128, 1024], BF16)
    wnat = sb("wnat", [128, 2048], F32)
    dqstage = sb("dqstage", [128, 1024], BF16)
    rr = sb("rr", [128, 1024], F32)

    R1 = max(NCD * S * 2,
             2 * (S + TOK) * 2 + 2 * KCH * 130 * 2 + 2 * 2 * QBS * 2,
             NCI * D * 2 + NCI * 128 * 4 + 2 * 2048)
    R2 = max(NCD * D * 2 + NCD * 2 * D * 2,
             NCD * I * 2,
             NCD * D * 2 + 8 * min(1024, TOK) * 4,
             NCD * D * 2 + 2 * NCI * 128 * 4)
    R3 = max(NCD * TOK * 2, 7 * D * 2)
    R4 = max(2 * NTB * HD * 4,
             4 * 2 * QBS * 2 + 2048 * 4,
             2 * NCI * 128 * 2)
    AR = ((R1 + 63) // 64 * 64) + ((R2 + 63) // 64 * 64) \
        + ((R3 + 63) // 64 * 64) + ((R4 + 63) // 64 * 64)
    arena = nc.alloc_sbuf_tensor("arena", [128, AR], mybir.dt.uint8).ap()
    o1 = 0
    o2 = o1 + (R1 + 63) // 64 * 64
    o3 = o2 + (R2 + 63) // 64 * 64
    o4 = o3 + (R3 + 63) // 64 * 64

    def carve(off, nbytes, dt, shape):
        ap = arena[:, off:off + nbytes].bitcast(dt)
        if len(shape) == 2:
            return ap
        if len(shape) == 3:
            return ap.rearrange("p (a b) -> p a b", b=shape[2])
        raise ValueError(shape)

    xnT = carve(o1, NCD * S * 2, BF16, [128, NCD, S])
    ktq = [carve(o1 + i * (S + TOK) * 2, (S + TOK) * 2, BF16,
                 [128, S + TOK]) for i in range(2)]
    _vo = o1 + 2 * (S + TOK) * 2
    vpair = [carve(_vo + i * KCH * 130 * 2, KCH * 130 * 2, BF16,
                   [128, KCH, 130]) for i in range(2)]
    _po = _vo + 2 * KCH * 130 * 2
    pt = [carve(_po + i * 2 * QBS * 2, 2 * QBS * 2, BF16,
                [128, 2 * QBS]) for i in range(2)]
    wddqT = carve(o1, NCI * D * 2, BF16, [128, NCI, D])

    wvT = carve(o2, NCD * D * 2, BF16, [128, NCD, D])
    wqkT = carve(o2 + NCD * D * 2, NCD * 2 * D * 2, BF16, [128, NCD, 2 * D])
    wdqT = carve(o2, NCD * I * 2, BF16, [128, NCD, I])
    woT_sb = carve(o2, NCD * D * 2, BF16, [128, NCD, D])
    _t2o = o2 + NCD * D * 2
    t2sl = [carve(_t2o + i * (NCI * 128 * 4), NCI * 128 * 4, F32,
                  [128, NCI, 128]) for i in range(2)]
    t2sl.append(carve(o1 + NCI * D * 2, NCI * 128 * 4, F32, [128, NCI, 128]))
    osl = [carve(o4 + i * (NCI * 128 * 2), NCI * 128 * 2,
                 BF16, [128, NCI, 128]) for i in range(2)]

    x_q = carve(o3, NCD * TOK * 2, BF16, [128, NCD, TOK])
    o_attn = carve(o3, NCD * TOK * 2, BF16, [128, NCD, TOK])
    xstg = carve(o3, D * 2, BF16, [128, D])
    xstg2 = carve(o3 + D * 2, D * 2, BF16, [128, D])
    xstg3 = carve(o3 + 2 * D * 2, D * 2, BF16, [128, D])
    sqb = [carve(o3 + (3 + j) * D * 2, D * 2, BF16, [128, D])
           for j in range(4)]

    ctab = carve(o4, NTB * HD * 4, F32, [128, NTB, HD])
    stab = carve(o4 + NTB * HD * 4, NTB * HD * 4, F32, [128, NTB, HD])
    oring = [carve(o4 + i * 2 * QBS * 2, 2 * QBS * 2, BF16, [128, 2 * QBS])
             for i in range(4)]
    nsc = carve(o4 + 4 * 2 * QBS * 2, 2048 * 4, F32, [128, 2048])
    PW = min(1024, TOK)
    _p5 = o2 + NCD * D * 2
    ph5_in = [carve(_p5 + i * PW * 4, PW * 4, F32, [128, PW]) for i in range(4)]
    ph5_out = [carve(_p5 + (4 + i) * PW * 4, PW * 4, F32, [128, PW]) for i in range(4)]
    ph5_t1 = ph5_in
    ph5_up = ph5_out
    ph5_t2 = ph5_in

    ps = lambda nm, shp: nc.alloc_psum_tensor(nm, shp, F32).ap()
    P0 = ps("P0", [128, 1024])
    P1 = ps("P1", [128, 1024])
    P2 = ps("P2", [128, 1024])
    P34 = ps("P34", [128, 1024])

    rrow = [0]

    def preduce(src, ncols, dst_scal_off, dv_tok=None):
        base = rrow[0]
        rrow[0] += ncols
        p.wait("DV", "DV")
        p.wait("SY", "DV", dv_tok)
        for j in range(ncols):
            p.dma("prs", red_scr[base + j:base + j + 1, :],
                  src[:, j:j + 1])
        p.wait("SY", "prs")
        done = 0
        prev_dv = None
        while done < ncols:
            if prev_dv is not None:
                p.wait("SY", "DV", prev_dv)
            nb = min(8, ncols - done)
            tok = p.dma("prl", sA[0:1, 0:nb * 128],
                        red_scr[base + done:base + done + nb, :]
                        .rearrange("k p -> (k p)").unsqueeze(0))
            p.wait("DV", "prl", tok)
            dv = p.DV("tensor_reduce",
                      scal[0:1, dst_scal_off + done:dst_scal_off + done + nb]
                      .unsqueeze(2),
                      sA[0:1, 0:nb * 128].rearrange("o (k p) -> o k p", p=128),
                      AX.X, ALU.max)
            p.wait("DV", "DV", dv)
            prev_dv = dv
            done += nb

    def bcast_scal(src_off, n, dst_ap, stream, dv_tok=None):
        base = rrow[0]
        rrow[0] += 1
        p.wait("SY", "DV", dv_tok)
        p.dma("prs", red_scr[base:base + 1, 0:n], scal[0:1, src_off:src_off + n])
        p.wait("SY", "prs")
        p.dma(stream, dst_ap, red_scr[base:base + 1, 0:n].broadcast_to((128, n)),
              allow_slow_non_contiguous=True)

    p.dma("ld0", ones_sb, ones_row)
    p.dma("ld0", id64_sb, id64c)
    p.dma("ld0", onesP64, onesP64c)
    p.dma("ld0", onesB64, onesB64c)
    p.dma("ld0", bqkv_sb, bqkv)
    p.dma("ld0", bo_sb, bo)

    _dq_last_store = [None]
    _dq_d6h = []

    def make_dq(w_in, rows, cols, out_scr, stream):
        ntile = rows // 128
        SLW = min(1024, cols)
        nblk = SLW // 32
        st = {"idx": 0, "d4": {}, "st": {}, "ld": {}}
        total = ntile * (cols // SLW)

        def step():
            idx = st["idx"]
            if idx > total:
                return False
            if idx < total:
                r, cs = divmod(idx, cols // SLW)
                cs *= SLW
                wt = wnat[:, 0:1024] if idx % 2 == 0 else wnat[:, 1024:2048]
                if idx >= 2:
                    p.wait("PL", "PL", st["d4"][idx - 2])
                st["ld"][idx] = p.dma(stream + str(idx % 2), wt[:, 0:SLW],
                                      w_in[r * 128:(r + 1) * 128, cs:cs + SLW],
                                      q="PL")
            if idx >= 1:
                jdx = idx - 1
                r, cs = divmod(jdx, cols // SLW)
                cs *= SLW
                wt = wnat[:, 0:1024] if jdx % 2 == 0 else wnat[:, 1024:2048]
                sEb = sE[:, 0:nblk] if jdx % 2 == 0 else sE[:, nblk:2 * nblk]
                sFb = sF[:, 0:nblk] if jdx % 2 == 0 else sF[:, nblk:2 * nblk]
                p.wait("DV", stream + str(jdx % 2), st["ld"][jdx])
                if len(_dq_d6h) >= 2:
                    p.wait("DV", "PL", _dq_d6h[-2])
                p.DV("tensor_reduce", sEb.unsqueeze(2),
                     wt[:, 0:SLW].rearrange("p (b e) -> p b e", e=32),
                     AX.X, ALU.max, apply_absolute_value=True)
                p.DV("tensor_scalar", sEb, sEb,
                     1.0 / 7.0, 1e-12, ALU.mult, ALU.max)
                d3 = p.DV("reciprocal", sFb, sEb)
                p.wait("PL", "DV", d3)
                d4 = p.PL("tensor_tensor",
                          rr[:, 0:SLW].rearrange("p (b e) -> p b e", e=32),
                          wt[:, 0:SLW].rearrange("p (b e) -> p b e", e=32),
                          sFb.unsqueeze(2)
                          .broadcast_to((128, nblk, 32)), ALU.mult)
                st["d4"][jdx] = d4
                p.wait("DV", "PL", d4)
                d5 = p.DV("tensor_scalar", rr[:, 0:SLW], rr[:, 0:SLW],
                          C_RNE, C_RNE, ALU.add, ALU.subtract)
                p.wait("PL", "DV", d5)
                if _dq_last_store[0] is not None:
                    p.wait("PL", _dq_last_store[0][0], _dq_last_store[0][1])
                d6 = p.PL("tensor_tensor",
                          dqstage[:, 0:SLW].rearrange("p (b e) -> p b e", e=32),
                          rr[:, 0:SLW].rearrange("p (b e) -> p b e", e=32),
                          sEb.unsqueeze(2)
                          .broadcast_to((128, nblk, 32)), ALU.mult)
                p.wait("PL", "PL", d6)
                _dq_d6h.append(d6)
                st["st"][jdx] = p.dma("stwq" + stream,
                                      out_scr[r * 128:(r + 1) * 128,
                                              cs:cs + SLW],
                                      dqstage[:, 0:SLW], q="PL")
                _dq_last_store[0] = ("stwq" + stream, st["st"][jdx])
            st["idx"] += 1
            return True
        return step

    _dqs = [make_dq(wu_n, I, D, wdq_scr, "ldwu"),
            make_dq(wg_n, I, D, wdqg_scr, "ldwg"),
            make_dq(wd_n, D, I, wddq_scr, "ldwd")]

    def dq_step(n=1):
        k = 0
        while k < n and _dqs:
            if not _dqs[0]():
                _dqs.pop(0)
                continue
            k += 1

    def dq_finish():
        while _dqs:
            dq_step(4)

    stx = {}
    ph1_ac = {}
    ph1_sum = {}
    ph1_ld = {}
    xbufs = [bfA, bfB, sC[:, 0:512].bitcast(BF16), sC[:, 512:1024].bitcast(BF16)]

    def ph1_A0(i):
        xt = xbufs[i % 4]
        if i >= 4:
            p.wait("SY", "AC", stx[i - 4][0])
        ph1_ld[i] = p.dma(f"ldx{i % 4}", xt[:, 0:D],
                          xb[i * 128:(i + 1) * 128, :])

    def ph1_A1(i):
        xt = xbufs[i % 4]
        sq = sA if i % 2 == 0 else sB
        p.wait("AC", f"ldx{i % 4}", ph1_ld[i])
        p.AC("activation", sq[:, 0:D], xt[:, 0:D], AF.Square,
             accum_out=stat[:, NTB + i:NTB + i + 1])
        ph1_sum[i] = p.AC("activation", sq[:, 0:D], xt[:, 0:D], AF.Copy,
                          accum_out=stat[:, i:i + 1])

    def ph1_A2(i):
        mci = stat[:, 2 * NTB + i:2 * NTB + i + 1]
        vci = stat[:, 3 * NTB + i:3 * NTB + i + 1]
        rsi = stat[:, 4 * NTB + i:4 * NTB + i + 1]
        p.wait("DV", "AC", ph1_sum[i])
        p.DV("tensor_scalar", mci, stat[:, i:i + 1], -1.0 / D, None, ALU.mult)
        p.DV("tensor_mul", vci, mci, mci)
        p.DV("scalar_tensor_tensor", vci, stat[:, NTB + i:NTB + i + 1],
             1.0 / D, vci, ALU.mult, ALU.subtract)
        dvv = p.DV("tensor_scalar", vci, vci, EPS_LN, None, ALU.add)
        p.wait("AC", "DV", dvv)
        ph1_ac[i] = p.AC("activation", rsi, vci, AF.Sqrt)

    def ph1_B(i):
        xt = xbufs[i % 4]
        mci = stat[:, 2 * NTB + i:2 * NTB + i + 1]
        rsi = stat[:, 4 * NTB + i:4 * NTB + i + 1]
        bci = stat[:, 5 * NTB + i:5 * NTB + i + 1]
        p.wait("DV", "AC", ph1_ac[i])
        p.DV("reciprocal", rsi, rsi)
        dvb = p.DV("tensor_mul", bci, rsi, mci)
        xo = bfC if i % 2 == 0 else xstg
        p.wait("AC", "DV", dvb)
        if i >= 2:
            p.wait("AC", "stxn%d" % (i % 2), stx[i - 2][1])
        dvx = p.AC("activation", xo[:, 0:D], xt[:, 0:D], AF.Identity,
                   bias=bci, scale=rsi)
        if i >= 1:
            pxo = bfC if (i - 1) % 2 == 0 else xstg
            p.wait("DV", "AC", stx[i - 1][0])
            p.DV("tensor_reduce", stat[:, 6 * NTB + i - 1:6 * NTB + i]
                 .unsqueeze(2), pxo[:, 0:D].unsqueeze(1), AX.X, ALU.max,
                 apply_absolute_value=True)
        p.wait("SY", "AC", dvx)
        st = p.dma("stxn%d" % (i % 2), xn_scr[i * 128:(i + 1) * 128, :],
                   xo[:, 0:D])
        stx[i] = (dvx, st)

    vtok = {}
    xnt_tok = {}

    TBS = 4 if NTB >= 4 else NTB
    TBW = TBS * 128

    def tblock_T(tb):
        p.wait("SY", "stxn0", stx[TBS * tb + TBS - 2][1])
        p.wait("SY", "stxn1", stx[TBS * tb + TBS - 1][1])
        toks = []
        for c in range(NCD):
            snm = "ldxnT%d" % (2 * (tb % 2) + c % 2)
            tok = p.dmat(snm, xnT[:, c, tb * TBW:(tb + 1) * TBW],
                         xn_scr[tb * TBW:(tb + 1) * TBW,
                                c * 128:(c + 1) * 128])
            toks.append((snm, tok))
        xnt_tok[tb] = toks[-2:]

    vpe = {}

    def ph2v_mm(i):
        pv = P0 if i % 2 == 0 else P1
        if i >= 2:
            p.wait("PE", "DV", vtok[i - 2][0])
        for snm, tok in xnt_tok[i // TBS]:
            p.wait("PE", snm, tok)
        for c in range(NCD):
            p.PE("matmul", pv[:, 0:512], xnT[:, c, i * 128:(i + 1) * 128],
                 wvT[:, c, 0:512], start=(c == 0), stop=False)
            p.PE("matmul", pv[:, 512:1024], xnT[:, c, i * 128:(i + 1) * 128],
                 wvT[:, c, 512:1024], start=(c == 0), stop=False)
        p.PE("matmul", pv[:, 0:512], ones_sb[0:1, :],
             bqkv_sb[0:1, 2 * D:2 * D + 512], start=False, stop=True)
        vpe[i] = p.PE("matmul", pv[:, 512:1024], ones_sb[0:1, :],
                      bqkv_sb[0:1, 2 * D + 512:3 * D], start=False, stop=True)

    def ph2v_ev(i):
        pv = P0 if i % 2 == 0 else P1
        p.wait("DV", "PE", vpe[i])
        vbuf = xstg2 if i % 2 == 0 else xstg3
        if i >= 2:
            p.wait("DV", "stv%d" % (i % 2), vtok[i - 2][1])
        dvv = p.DV("tensor_copy", vbuf[:, 0:D], pv[:, 0:1024])
        p.wait("SY", "DV", dvv)
        st = p.dma("stv%d" % (i % 2), v_scr[i * 128:(i + 1) * 128, :],
                   vbuf[:, 0:D])
        vtok[i] = (dvv, st)

    def ph2v_chunk(i):
        ph2v_mm(i)
        if i >= 1:
            ph2v_ev(i - 1)

    for c in range(NCD):
        p.dma("ldwv", wvT[:, c, :], wqkvT[c * 128:(c + 1) * 128, 2 * D:3 * D])
    p.wait("PE", "ldwv")
    p.wait("PE", "ld0")
    NTBL = NTB // TBS
    ph1_A0(0)
    if NTB > 1:
        ph1_A0(1)
    for s in range(NTB + 2):
        if s >= 2:
            b = s - 2
            ph1_B(b)
            dq_step(1)
        if s + 2 < NTB:
            ph1_A0(s + 2)
        if s >= 2:
            b = s - 2
            if b % TBS == TBS - 1:
                tblock_T(b // TBS)
            if b % TBS == 1 and b >= TBS:
                for j in range(TBS * (b // TBS - 1), TBS * (b // TBS)):
                    ph2v_chunk(j)
                    dq_step(1)
        if 1 <= s <= NTB:
            ph1_A2(s - 1)
        if s < NTB:
            ph1_A1(s)
    p.wait("DV", "AC", stx[NTB - 1][0])
    p.DV("tensor_reduce", stat[:, 7 * NTB - 1:7 * NTB].unsqueeze(2),
         (bfC if (NTB - 1) % 2 == 0 else xstg)[:, 0:D].unsqueeze(1),
         AX.X, ALU.max, apply_absolute_value=True)
    for j in range(TBS * (NTBL - 1), NTB):
        ph2v_chunk(j)
        dq_step(1)
    ph2v_ev(NTB - 1)

    p.dma("ldtab", ctab, tabc.rearrange("(t p) d -> p t d", p=128))
    p.dma("ldtab", stab, tabs.rearrange("(t p) d -> p t d", p=128))
    for c in range(NCD):
        p.dma("ldwqk", wqkT[:, c, 0:D], wqkvT[c * 128:(c + 1) * 128, 0:D])
        p.dma("ldwqk", wqkT[:, c, D:2 * D], wqkvT[c * 128:(c + 1) * 128, D:2 * D])
    p.wait("PE", "ldwqk")
    p.wait("PE", "DV", vtok[NTB - 1][0])
    p.wait("DV", "ldtab")
    p.wait("DV", "stv0")
    p.wait("DV", "stv1")
    dv = p.DV("memset", nrm[:, 0:32], 0.0)
    p.wait("DV", "DV", dv)
    qk_dv = {}
    stk_tok = {}
    stq_tok = {}

    def rope(psrc, dst_bf, pe_tok):
        p.wait("DV", "PE", pe_tok)
        v3 = psrc.rearrange("p (h two q) -> p h two q", two=2, q=HD // 2)
        vswap = v3[:, :, ::-1, :]
        cosb = ctab[:, rope_i[0], :].rearrange("p (two q) -> p two q", two=2) \
            .unsqueeze(1).broadcast_to((128, H, 2, HD // 2))
        sinb = stab[:, rope_i[0], :].rearrange("p (two q) -> p two q", two=2) \
            .unsqueeze(1).broadcast_to((128, H, 2, HD // 2))
        d1 = p.DV("tensor_tensor", sA[:, 0:D].rearrange(
            "p (h two q) -> p h two q", two=2, q=HD // 2), v3, cosb, ALU.mult)
        p.wait("DV", "DV", d1)
        d2 = p.DV("tensor_tensor", sB[:, 0:D].rearrange(
            "p (h two q) -> p h two q", two=2, q=HD // 2), vswap, sinb, ALU.mult)
        p.wait("DV", "DV", d2)
        d3 = p.DV("tensor_add", dst_bf[:, 0:D], sA[:, 0:D], sB[:, 0:D])
        p.wait("DV", "DV", d3)
        return d3

    rope_i = [0]
    ack_k, ack_q, red_k, red_q = {}, {}, {}, {}

    def norm_red(j):
        p.wait("DV", "AC", ack_k[j])
        p.DV("tensor_reduce", sD[:, 0:16].unsqueeze(2),
             sqb[(2 * j) % 4][:, 0:D].rearrange("p (h d) -> p h d", d=HD),
             AX.X, ALU.add)
        red_k[j] = p.DV("tensor_max", nrm[:, 16:32], nrm[:, 16:32],
                        sD[:, 0:16])
        if j < NTM:
            p.wait("DV", "AC", ack_q[j])
            p.DV("tensor_reduce", sD[:, 0:16].unsqueeze(2),
                 sqb[(2 * j + 1) % 4][:, 0:D]
                 .rearrange("p (h d) -> p h d", d=HD), AX.X, ALU.add)
            red_q[j] = p.DV("tensor_max", nrm[:, 0:16], nrm[:, 0:16],
                            sD[:, 0:16])

    for i in range(NTB):
        rope_i[0] = i
        mine = i < NTM
        pk = P0 if i % 2 == 0 else P1
        pq = P2 if i % 2 == 0 else P34
        if i >= 2:
            p.wait("PE", "DV", qk_dv[i - 2])
        for c in range(NCD):
            p.PE("matmul", pk[:, 0:512], xnT[:, c, i * 128:(i + 1) * 128],
                 wqkT[:, c, D:D + 512], start=(c == 0), stop=False)
            p.PE("matmul", pk[:, 512:1024], xnT[:, c, i * 128:(i + 1) * 128],
                 wqkT[:, c, D + 512:2 * D], start=(c == 0), stop=False)
            if mine:
                p.PE("matmul", pq[:, 0:512], xnT[:, c, i * 128:(i + 1) * 128],
                     wqkT[:, c, 0:512], start=(c == 0), stop=False)
                p.PE("matmul", pq[:, 512:1024], xnT[:, c, i * 128:(i + 1) * 128],
                     wqkT[:, c, 512:1024], start=(c == 0), stop=False)
        p.PE("matmul", pk[:, 0:512], ones_sb[0:1, :], bqkv_sb[0:1, D:D + 512],
             start=False, stop=True)
        pek = p.PE("matmul", pk[:, 512:1024], ones_sb[0:1, :],
                   bqkv_sb[0:1, D + 512:2 * D], start=False, stop=True)
        peq = None
        if mine:
            p.PE("matmul", pq[:, 0:512], ones_sb[0:1, :], bqkv_sb[0:1, 0:512],
                 start=False, stop=True)
            peq = p.PE("matmul", pq[:, 512:1024], ones_sb[0:1, :],
                       bqkv_sb[0:1, 512:1024], start=False, stop=True)
        kdst = bfA if i % 2 == 0 else xstg2
        if i >= 2:
            p.wait("DV", "stk%d" % (i % 2), stk_tok[i - 2])
        d3k = rope(pk, kdst, pek)
        p.wait("AC", "DV", d3k)
        if i >= 2:
            p.wait("AC", "DV", red_k[i - 2])
        ack_k[i] = p.AC("activation", sqb[(2 * i) % 4][:, 0:D],
                        kdst[:, 0:D], AF.Square)
        p.wait("SY", "DV", d3k)
        stk_tok[i] = p.dma("stk%d" % (i % 2), k_scr[i * 128:(i + 1) * 128, :],
                           kdst[:, 0:D])
        if mine:
            qdst = bfB if i % 2 == 0 else xstg3
            if i >= 2:
                p.wait("DV", "stq%d" % (i % 2), stq_tok[i - 2])
            d3q = rope(pq, qdst, peq)
            p.wait("AC", "DV", d3q)
            if i >= 2:
                p.wait("AC", "DV", red_q[i - 2])
            ack_q[i] = p.AC("activation", sqb[(2 * i + 1) % 4][:, 0:D],
                            qdst[:, 0:D], AF.Square)
            p.wait("SY", "DV", d3q)
            stq_tok[i] = p.dma("stq%d" % (i % 2),
                               q_scr[i * 128:(i + 1) * 128, :], qdst[:, 0:D])
        if i >= 1:
            norm_red(i - 1)
        qk_dv[i] = p.cnt["DV"]
        dq_step(1)
    norm_red(NTB - 1)

    preduce(nrm[:, 0:32], 32, 16)
    dv = p.DV("tensor_add", scal[0:1, 48:64], scal[0:1, 16:32], scal[0:1, 32:48])
    p.wait("DV", "DV", dv)
    s2v = scal[0:1, 48:64].rearrange("o (h two) -> o h two", two=2)
    dv = p.DV("tensor_max", scal[0:1, 48:56], s2v[:, :, 0:1].squeeze(2),
              s2v[:, :, 1:2].squeeze(2))
    p.wait("DV", "DV", dv)
    dv = p.DV("tensor_scalar", scal[0:1, 56:64], scal[0:1, 48:56],
              -0.502, -0.01, ALU.mult, ALU.add)
    p.wait("DV", "DV", dv)
    bcast_scal(56, 8, scbc[:, 8:16], "bc2")

    maxc = stat[:, 6 * NTB:7 * NTB]
    tmpc = stat[:, 7 * NTB:8 * NTB]
    p.wait("DV", "stxn0")
    p.wait("DV", "stxn1")
    p.wait("DV", "DV")
    dv = p.DV("tensor_reduce", tmpc[:, 0:1].unsqueeze(2),
              maxc.unsqueeze(1), AX.X, ALU.max)
    preduce(tmpc[:, 0:1], 1, 0)
    dv = p.DV("tensor_scalar", scal[0:1, 1:2], scal[0:1, 0:1],
              1.0 / QMAXI, 1e-12, ALU.mult, ALU.max)
    p.wait("DV", "DV", dv)
    dv = p.DV("tensor_copy", bfC[0:1, 0:1], scal[0:1, 1:2])
    p.wait("DV", "DV", dv)
    dv = p.DV("tensor_copy", scal[0:1, 2:3], bfC[0:1, 0:1])
    p.wait("DV", "DV", dv)
    dv = p.DV("reciprocal", scal[0:1, 3:4], scal[0:1, 2:3])
    p.wait("DV", "DV", dv)
    bcast_scal(2, 2, scbc[:, 0:2], "bc1", dv_tok=dv)

    p.wait("DV", "bc1")
    p.wait("DV", "ldxnT")
    p.wait("DV", "stxn0")
    p.wait("DV", "stxn1")
    p.wait("DV", "AC")
    for _k in range(2):
        p.wait("DV", "stk%d" % _k)
        p.wait("DV", "stq%d" % _k)
    SL = min(1024, TOK)
    for c in range(NCD):
        for t0 in range(0, TOK, SL):
            d1 = p.DV("tensor_scalar", bfC[:, 0:SL], xnT[:, c, t0:t0 + SL],
                      scbc[:, 1:2], None, ALU.mult)
            p.wait("DV", "DV", d1)
            d2 = p.DV("tensor_scalar", sA[:, 0:SL], bfC[:, 0:SL],
                      C_RNE, None, ALU.add)
            p.wait("DV", "DV", d2)
            d3 = p.DV("tensor_scalar", x_q[:, c, t0:t0 + SL], sA[:, 0:SL],
                      C_RNE, scbc[:, 0:1], ALU.subtract, ALU.mult)
            p.wait("DV", "DV", d3)

    st_tok = {}
    fidx = [0]
    IH = I // 2
    pus = [P2[:, 0:512], P2[:, 512:1024], P34[:, 0:512], P34[:, 512:1024]]
    ebs = [sA, sB, sC, sD]

    def ffn_T(dq_scr, stream, h, pe_tok):
        p.wait("SY", "stwq" + stream)
        p.wait("SY", "PE", pe_tok)
        for c in range(NCD):
            p.dmat("ldwt" + stream + str(h),
                   wdqT[:, c, h * IH:(h + 1) * IH],
                   dq_scr[h * IH:(h + 1) * IH, c * 128:(c + 1) * 128])

    def ffn_half(out_scr, maxslot, stream, h):
        p.wait("PE", "ldwt" + stream + str(h))
        dvz = p.DV("memset", nrm[:, maxslot:maxslot + 1], 0.0) \
            if h == 0 else None
        if dvz is not None:
            p.wait("DV", "DV", dvz)
        pe = None
        for jj in range(h * (NCI // 2), (h + 1) * (NCI // 2)):
            for g in range(NTG):
                idx = fidx[0]
                fidx[0] += 1
                pu = pus[idx % 4]
                if idx >= 4:
                    p.wait("PE", "DV", st_tok[idx - 4][0])
                for c in range(NCD):
                    pe = p.PE("matmul", pu[:, 0:GSZ],
                              wdqT[:, c, jj * 128:(jj + 1) * 128],
                              x_q[:, c, g * GSZ:(g + 1) * GSZ],
                              start=(c == 0), stop=(c == NCD - 1))
                p.wait("DV", "PE", pe)
                eb = ebs[idx % 4]
                if idx >= 4:
                    p.wait("DV", st_tok[idx - 4][2], st_tok[idx - 4][1])
                d1 = p.DV("tensor_copy", eb[:, 0:GSZ], pu[:, 0:GSZ])
                p.DV("tensor_reduce", nrm[:, 42:43].unsqueeze(2),
                     eb[:, 0:GSZ].unsqueeze(1), AX.X, ALU.max,
                     apply_absolute_value=True)
                dacc = p.DV("tensor_max", nrm[:, maxslot:maxslot + 1],
                            nrm[:, maxslot:maxslot + 1], nrm[:, 42:43])
                p.wait("DV", "DV", dacc)
                p.wait("SY", "DV", d1)
                st = p.dma(stream + "o%d" % (idx % 4),
                           out_scr[jj * 128:(jj + 1) * 128,
                                   g * GSZ:(g + 1) * GSZ],
                           eb[:, 0:GSZ])
                st_tok[idx] = (p.cnt["DV"], st, stream + "o%d" % (idx % 4))
        return pe

    dq_finish()
    p.wait("SY", "PE")
    ffn_T(wdq_scr, "ldwu", 0, None)
    ffn_T(wdq_scr, "ldwu", 1, None)
    p.wait("PE", "DV")
    pe0 = ffn_half(up_scr, 32, "ldwu", 0)
    ffn_T(wdqg_scr, "ldwg", 0, pe0)
    pe1 = ffn_half(up_scr, 32, "ldwu", 1)
    ffn_T(wdqg_scr, "ldwg", 1, pe1)
    ffn_half(gate_scr, 33, "ldwg", 0)
    ffn_half(gate_scr, 33, "ldwg", 1)
    for _k in range(4):
        p.wait("SY", "ldwuo%d" % _k)
        p.wait("SY", "ldwgo%d" % _k)
    preduce(nrm[:, 32:34], 2, 4)
    dv = p.DV("tensor_scalar", scal[0:1, 6:8], scal[0:1, 4:6],
              1.0 / QMAXI, 1e-12, ALU.mult, ALU.max)
    p.wait("DV", "DV", dv)
    dv = p.DV("reciprocal", scal[0:1, 8:10], scal[0:1, 6:8])
    p.wait("DV", "DV", dv)
    bcast_scal(6, 4, scbc[:, 2:6], "bc3")

    PWN = (I * TOK) // (128 * PW)
    NGW = TOK // PW
    import heapq
    sched = []
    scnt = [0]
    seqc = [0]

    def schedule(delay, fn):
        seqc[0] += 1
        heapq.heappush(sched, (scnt[0] + delay, seqc[0], fn))

    def slot():
        scnt[0] += 1
        if sched and sched[0][0] <= scnt[0]:
            heapq.heappop(sched)[2]()

    def drain():
        while sched:
            slot()

    pa_ld, pa_q2, pa_a1, pa_d5, pa_st = {}, {}, {}, {}, {}
    pa_acc = [None]
    pa2_done = [0]
    pb_l1, pb_l2, pb_d5, pb_st, pb_red = {}, {}, {}, {}, {}
    pb_acc = [None]
    pb2_done = [0]

    def pa_load(t):
        gb = ph5_in[t % 4]
        jj, g = t // NGW, t % NGW
        if t < 4:
            for _k in range(4):
                p.wait("SY", "ldwgo%d" % _k)
        else:
            p.wait("SY", "DV", pa_d5[t - 4])
        pa_ld[t] = p.dma("ldg%d" % (t % 4), gb,
                         gate_scr[jj * 128:(jj + 1) * 128, g * PW:(g + 1) * PW])
        schedule(6, lambda: pa_quant(t))

    def pa_quant(t):
        gb = ph5_in[t % 4]
        tmp = sC if t % 2 == 0 else sB
        jj, g = t // NGW, t % NGW
        if t >= 2 and (t - 2) not in pa_d5:
            schedule(1, lambda: pa_quant(t))
            return
        if t < 4:
            p.wait("DV", "bc3")
        p.wait("DV", "ldg%d" % (t % 4), pa_ld[t])
        p.DV("tensor_scalar", tmp[:, 0:PW], gb, scbc[:, 5:6], C_RNE,
             ALU.mult, ALU.add)
        pa_q2[t] = p.DV("tensor_scalar", gb, tmp[:, 0:PW], C_RNE, scbc[:, 3:4],
                        ALU.subtract, ALU.mult)
        schedule(3, lambda: pa_act(t))

    def pa_act(t):
        gb = ph5_in[t % 4]
        tmp = sC if t % 2 == 0 else sB
        p.wait("AC", "DV", pa_q2[t])
        pa_a1[t] = p.AC("activation", tmp[:, 0:PW], gb, AF.Tanh, scale=0.5)
        schedule(2, lambda: pa_fin(t))

    def pa_fin(t):
        gb = ph5_in[t % 4]
        ob = ph5_out[t % 4]
        tmp = sC if t % 2 == 0 else sB
        jj, g = t // NGW, t % NGW
        p.wait("DV", "AC", pa_a1[t])
        p.DV("tensor_scalar", tmp[:, 0:PW], tmp[:, 0:PW],
             32768.0, 32768.0 + C_RNE, ALU.mult, ALU.add)
        p.DV("tensor_scalar", tmp[:, 0:PW], tmp[:, 0:PW],
             C_RNE, 65535.0, ALU.subtract, ALU.min)
        if t >= 4:
            p.wait("DV", "stt1%d" % (t % 4), pa_st[t - 4])
        d5 = p.DV("scalar_tensor_tensor", ob, tmp[:, 0:PW],
                  float(2.0 ** -16), gb, ALU.mult, ALU.mult)
        p.DV("tensor_reduce", sF[:, 0:1].unsqueeze(2), ob.unsqueeze(1),
             AX.X, ALU.max, apply_absolute_value=True)
        pa_acc[0] = p.DV("tensor_max", nrm[:, 34:35], nrm[:, 34:35],
                         sF[:, 0:1])
        pa_d5[t] = d5
        p.wait("SY", "DV", d5)
        pa_st[t] = p.dma("stt1%d" % (t % 4),
                         t1_scr[jj * 128:(jj + 1) * 128, g * PW:(g + 1) * PW],
                         ob)
        pa2_done[0] += 1
        if t + 4 < PWN:
            schedule(31, lambda: pa_load(t + 4))
        if pa2_done[0] == PWN:
            schedule(2, m4_resolve)

    def m4_resolve():
        preduce(nrm[:, 34:35], 1, 10, dv_tok=pa_acc[0])
        dv = p.DV("tensor_scalar", scal[0:1, 11:12], scal[0:1, 10:11],
                  1.0 / QMAXI, 1e-12, ALU.mult, ALU.max)
        p.wait("DV", "DV", dv)
        dv = p.DV("reciprocal", scal[0:1, 12:13], scal[0:1, 11:12])
        p.wait("DV", "DV", dv)
        bcast_scal(11, 2, scbc[:, 6:8], "bc4", dv_tok=dv)
        p.DV("memset", nrm[:, 36:37], 0.0)
        for _t in range(min(4, PWN)):
            schedule(2 + 3 * _t, (lambda tt: lambda: pb_load(tt))(_t))

    def pb_load(t):
        tb = ph5_t1[t % 4]
        ub = ph5_up[t % 4]
        jj, g = t // NGW, t % NGW
        if t < 4:
            for _k in range(4):
                p.wait("SY", "ldwuo%d" % _k)
            p.wait("SY", "stt1%d" % (t % 4))
            p.wait("SY", "DV", pa_acc[0])
        else:
            p.wait("SY", "DV", pb_red[t - 4])
            p.wait("SY", "stt2%d" % (t % 4), pb_st[t - 4])
            p.wait("SY", "stt1%d" % (t % 4))
        pb_l1[t] = p.dma("ldt1%d" % (t % 4), tb,
                         t1_scr[jj * 128:(jj + 1) * 128, g * PW:(g + 1) * PW])
        pb_l2[t] = p.dma("ldu%d" % (t % 4), ub,
                         up_scr[jj * 128:(jj + 1) * 128, g * PW:(g + 1) * PW])
        schedule(9, lambda: pb_quant(t))

    def pb_quant(t):
        tb = ph5_t1[t % 4]
        ub = ph5_up[t % 4]
        jj, g = t // NGW, t % NGW
        if t < 4:
            p.wait("DV", "bc4")
        p.wait("DV", "ldt1%d" % (t % 4), pb_l1[t])
        p.wait("DV", "ldu%d" % (t % 4), pb_l2[t])
        p.DV("tensor_scalar", tb, tb, scbc[:, 7:8], C_RNE, ALU.mult, ALU.add)
        p.DV("tensor_scalar", tb, tb, C_RNE, scbc[:, 6:7],
             ALU.subtract, ALU.mult)
        p.DV("tensor_scalar", ub, ub, scbc[:, 4:5], C_RNE, ALU.mult, ALU.add)
        p.DV("tensor_scalar", ub, ub, C_RNE, scbc[:, 2:3],
             ALU.subtract, ALU.mult)
        d5 = p.DV("tensor_mul", tb, tb, ub)
        pb_red[t] = p.DV("tensor_reduce", sF[:, 0:1].unsqueeze(2),
                         tb.unsqueeze(1), AX.X, ALU.max,
                         apply_absolute_value=True)
        pb_acc[0] = p.DV("tensor_max", nrm[:, 36:37], nrm[:, 36:37],
                         sF[:, 0:1])
        pb_d5[t] = d5
        p.wait("SY", "DV", d5)
        for sub in range(PW // 128):
            tt = (g * PW) // 128 + sub
            p.dma("stt2%d" % (t % 4), t2_scr[tt, jj * 128:(jj + 1) * 128, :],
                  tb[:, sub * 128:(sub + 1) * 128])
        pb_st[t] = p.cnt["stt2%d" % (t % 4)]
        pb2_done[0] += 1
        if t + 4 < PWN:
            schedule(29, lambda: pb_load(t + 4))
        if pb2_done[0] == PWN:
            schedule(2, m5_resolve)

    def m5_resolve():
        preduce(nrm[:, 36:37], 1, 13, dv_tok=pb_acc[0])
        dv = p.DV("tensor_scalar", scal[0:1, 14:15], scal[0:1, 13:14],
                  1.0 / QMAXI, 1e-12, ALU.mult, ALU.max)
        p.wait("DV", "DV", dv)
        dv = p.DV("reciprocal", scal[0:1, 15:16], scal[0:1, 14:15])
        p.wait("DV", "DV", dv)
        bcast_scal(14, 2, nrm[:, 40:42], "bc5", dv_tok=dv)

    p.DV("memset", nrm[:, 34:35], 0.0)
    for _t in range(min(4, PWN)):
        schedule(1 + 3 * _t, (lambda tt: lambda: pa_load(tt))(_t))

    p.wait("SY", "PE")
    for c in range(NCD):
        p.dma("ldwo", woT_sb[:, c, :], woT[c * 128:(c + 1) * 128, :])
    for _k in range(2):
        p.wait("SY", "stk%d" % _k)
        p.wait("SY", "stq%d" % _k)
    p.wait("SY", "stv0")
    p.wait("SY", "stv1")

    def load_pair(pp, buf):
        b = ktq[buf]
        vb = vpair[buf]
        sn = "ldpair%d" % buf
        p.dmat(sn, b[:, 0:S], k_scr[:, pp * 128:(pp + 1) * 128])
        p.dmat(sn, b[:, S:S + TOK], q_scr[:, pp * 128:(pp + 1) * 128])
        for g in range(2):
            p.dma(sn, vb[:, :, g * 65:g * 65 + 64],
                  v_scr[:, pp * 128 + g * 64:pp * 128 + (g + 1) * 64]
                  .rearrange("(c p) e -> p c e", p=128))
        return p.cnt[sn]

    n2_pe, n3_dv, n4_pl, n5_pe, n6_dv, n1_dv = {}, {}, {}, {}, {}, {}

    def norm_chain(pp_, qb_, k_, ci, ev_tok):
        slab = oring[k_]
        rb = (ci % 2) * 2 * QBS
        r0 = 64 if ci % 2 == 0 else 32
        rc = [None]

        def n1():
            if ci >= 2 and (ci - 2) not in n2_pe:
                schedule(1, n1)
                return
            if ci >= 2:
                p.wait("DV", "PE", n2_pe[ci - 2])
            p.wait("DV", "AC", ev_tok)
            rc[0] = p.DV("reciprocal", rr[r0:r0 + 1, 0:2 * QBS],
                         nsc[64:65, rb:rb + 2 * QBS])
            n1_dv[ci] = rc[0]
            schedule(3, n2)

        def n2():
            if ci >= 1 and (ci - 1) not in n3_dv:
                schedule(1, n2)
                return
            p.wait("PE", "DV", rc[0])
            if ci >= 1:
                p.wait("PE", "DV", n3_dv[ci - 1])
            p.PE("matmul", P34[0:64, 0:QBS], onesP64[r0:r0 + 1, 0:64],
                 rr[r0:r0 + 1, 0:QBS], start=True, stop=True)
            n2_pe[ci] = p.PE("matmul", P34[0:64, QBS:2 * QBS],
                             onesP64[r0:r0 + 1, 0:64],
                             rr[r0:r0 + 1, QBS:2 * QBS], start=True, stop=True)
            schedule(1, n3)

        def n3():
            if ci >= 2 and (ci - 2) not in n4_pl:
                schedule(1, n3)
                return
            p.wait("DV", "PE", n2_pe[ci])
            if ci >= 2:
                p.wait("DV", "PL", n4_pl[ci - 2])
            n3_dv[ci] = p.DV("tensor_copy", nsc[0:64, rb:rb + 2 * QBS],
                             P34[0:64, 0:2 * QBS])
            schedule(1, n4)

        def n4():
            if ci >= 1 and (ci - 1) not in n5_pe:
                schedule(1, n4)
                return
            p.wait("PL", "DV", n3_dv[ci])
            p.wait("PL", "AC", ev_tok)
            if ci >= 1:
                p.wait("PL", "PE", n5_pe[ci - 1])
            p.PL("tensor_tensor", o_attn[0:64, pp_, qb_ * QBS:(qb_ + 1) * QBS],
                 slab[0:64, 0:QBS], nsc[0:64, rb:rb + QBS], ALU.mult)
            n4_pl[ci] = p.PL("tensor_tensor", bfB[0:64, 0:QBS],
                             slab[0:64, QBS:2 * QBS],
                             nsc[0:64, rb + QBS:rb + 2 * QBS], ALU.mult)
            schedule(1, n5)

        def n5():
            if ci >= 1 and (ci - 1) not in n6_dv:
                schedule(1, n5)
                return
            p.wait("PE", "PL", n4_pl[ci])
            if ci >= 1:
                p.wait("PE", "DV", n6_dv[ci - 1])
            n5_pe[ci] = p.PE("matmul", P34[64:128, 0:QBS], id64_sb[:, :],
                             bfB[0:64, 0:QBS], start=True, stop=True,
                             tile_position=(0, 64))
            schedule(1, n6)

        def n6():
            p.wait("DV", "PE", n5_pe[ci])
            n6_dv[ci] = p.DV("tensor_copy",
                             o_attn[64:128, pp_, qb_ * QBS:(qb_ + 1) * QBS],
                             P34[64:128, 0:QBS])

        return n1

    rq = [0]
    p.wait("DV", "PL")
    ld_tok = {0: load_pair(0, 0)}
    pe_after_pair = {}
    for pp in range(PAIRS):
        buf = pp % 2
        p.wait("DV", "ldpair%d" % buf, ld_tok[pp])
        dvo = p.DV("memset",
                   vpair[buf].rearrange("p c (g e) -> p c g e", g=2, e=65)
                   [:, :, :, 64:65], 1.0)
        p.wait("PE", "DV", dvo)
        p.wait("PE", "ldpair%d" % buf, ld_tok[pp])
        if pp + 1 < PAIRS:
            if pp >= 1:
                p.wait("SY", "PE", pe_after_pair[pp - 1])
            ld_tok[pp + 1] = load_pair(pp + 1, (pp + 1) % 2)
        b = ktq[buf]
        vb = vpair[buf]
        p.wait("AC", "bc2")
        for qb in range(NQB):
            exp_tok = {}
            pv_tok = {}
            sc_tok = {}

            def emit_scores(j):
                sc = P0 if j % 2 == 0 else P1
                if j >= 2:
                    p.wait("PE", "AC", exp_tok[j - 2])
                p.PE("matmul", sc[:, 0:QBS], b[0:64, j * 128:(j + 1) * 128],
                     b[0:64, S + qb * QBS:S + (qb + 1) * QBS],
                     start=True, stop=True, tile_position=(0, 0))
                sc_tok[j] = p.PE("matmul", sc[:, QBS:2 * QBS],
                                 b[64:128, j * 128:(j + 1) * 128],
                                 b[64:128, S + qb * QBS:S + (qb + 1) * QBS],
                                 start=True, stop=True, tile_position=(64, 0))

            emit_scores(0)
            for j in range(KCH):
                sc = P0 if j % 2 == 0 else P1
                ptb = pt[j % 2]
                p.wait("AC", "PE", sc_tok[j])
                if j >= 2:
                    p.wait("AC", "PE", pv_tok[j - 2])
                ace = p.AC("activation", ptb[:, 0:2 * QBS], sc[:, 0:2 * QBS],
                           AF.Exp, bias=scbc[:, 8 + pp:9 + pp], scale=1.0)
                exp_tok[j] = ace
                if j + 1 < KCH:
                    emit_scores(j + 1)
                p.wait("PE", "AC", ace)
                p.PE("matmul", P2[0:65, 0:QBS], vb[:, j, 0:65], ptb[:, 0:QBS],
                     start=(j == 0), stop=(j == KCH - 1))
                pev = p.PE("matmul", P2[0:65, 512:512 + QBS], vb[:, j, 65:130],
                           ptb[:, QBS:2 * QBS],
                           start=(j == 0), stop=(j == KCH - 1))
                pv_tok[j] = pev
                slot()
            ci = rq[0]
            k_ = ci % 4
            rq[0] += 1
            rb = (ci % 2) * 2 * QBS
            while ci >= 4 and (ci - 4) not in n4_pl:
                slot()
            while ci >= 2 and (ci - 2) not in n1_dv:
                slot()
            p.wait("DV", "PE", pv_tok[KCH - 1])
            p.DV("tensor_copy", nsc[64:65, rb:rb + QBS], P2[64:65, 0:QBS])
            devd = p.DV("tensor_copy", nsc[64:65, rb + QBS:rb + 2 * QBS],
                        P2[64:65, 512:512 + QBS])
            p.wait("AC", "PE", pv_tok[KCH - 1])
            if ci >= 4:
                p.wait("AC", "PL", n4_pl[ci - 4])
            p.AC("activation", oring[k_][0:64, 0:QBS], P2[0:64, 0:QBS],
                 AF.Copy)
            dev = p.AC("activation", oring[k_][0:64, QBS:2 * QBS],
                       P2[0:64, 512:512 + QBS], AF.Copy)
            schedule(2, norm_chain(pp, qb, k_, ci, dev))
            p.wait("PE", "AC", dev)
            p.wait("PE", "DV", devd)
        pe_after_pair[pp] = p.cnt["PE"]

    drain()
    p.wait("PE", "DV")
    p.wait("PE", "PL")
    if S <= 512:
        odbg = nc.dram_tensor("o_dbg", [NCD * 128, TOK], BF16, kind="Internal").ap()
        p.wait("SY", "DV")
        p.wait("SY", "PL")
        for c in range(NCD):
            p.dma("odbg", odbg[c * 128:(c + 1) * 128, :], o_attn[:, c, :])
        p.wait("SY", "odbg")
    p.wait("SY", "PE")
    p.wait("SY", "DV")
    p.wait("SY", "PL")
    p.wait("SY", "stwqldwd")
    for _k in range(4):
        p.wait("SY", "stt2%d" % _k)
        p.wait("SY", "ldt1%d" % _k)
        p.wait("SY", "ldu%d" % _k)
        p.wait("SY", "ldg%d" % _k)
        p.wait("SY", "stt1%d" % _k)
    t2_ld = {}
    NSP = NCI // 4

    def ldt2_issue(i):
        t2b = t2sl[i % 3]
        src = t2_scr[i].rearrange("(c p) t -> p c t", p=128)
        t2_ld[i] = [p.dma("ldt2%d_%d" % (i % 3, k),
                          t2b[:, k * NSP:(k + 1) * NSP, :],
                          src[:, k * NSP:(k + 1) * NSP, :], q="PL")
                    for k in range(4)]

    p.wait("PL", "PE")
    p.wait("PL", "DV")
    for _k in range(4):
        p.wait("PL", "stt2%d" % _k)
        p.wait("PL", "ldt1%d" % _k)
        p.wait("PL", "ldu%d" % _k)
        p.wait("PL", "ldg%d" % _k)
        p.wait("PL", "stt1%d" % _k)
    for i in range(min(3, NTM)):
        ldt2_issue(i)
    p.wait("AC", "PE")
    p.wait("AC", "stwqldwd")
    for c in range(NCI):
        for h in range(2):
            p.dmat("ldwdt%d" % ((2 * c + h) % 6),
                   wddqT[:, c, h * (D // 2):(h + 1) * (D // 2)],
                   wddq_scr[h * (D // 2):(h + 1) * (D // 2),
                            c * 128:(c + 1) * 128],
                   q="SY")

    woev = [carve(o1 + NCI * D * 2 + NCI * 128 * 4 + k * 2048, 2048, BF16,
                  [128, 1024]) for k in range(2)]
    p.wait("PE", "ldwo")
    p.wait("PE", "AC")
    p.wait("PE", "DV")
    wo_ev = {}
    fin = {}
    fin_ld = {}
    fin_d2 = {}

    def wo_chunk(i):
        pw = P0 if i % 2 == 0 else P1
        if i >= 2:
            p.wait("PE", "DV", wo_ev[i - 2][0])
        for c in range(NCD):
            p.PE("matmul", pw[:, 0:512], o_attn[:, c, i * 128:(i + 1) * 128],
                 woT_sb[:, c, 0:512], start=(c == 0), stop=False)
            p.PE("matmul", pw[:, 512:1024], o_attn[:, c, i * 128:(i + 1) * 128],
                 woT_sb[:, c, 512:1024], start=(c == 0), stop=False)
        p.PE("matmul", pw[:, 0:512], ones_sb[0:1, :], bo_sb[0:1, 0:512],
             start=False, stop=True)
        pe = p.PE("matmul", pw[:, 512:1024], ones_sb[0:1, :],
                  bo_sb[0:1, 512:1024], start=False, stop=True)
        p.wait("DV", "PE", pe)
        eb = woev[i % 2]
        if i >= 2:
            p.wait("DV", "sts%d" % (i % 2), wo_ev[i - 2][1])
        dv = p.DV("tensor_copy", eb, pw[:, 0:1024])
        p.wait("SY", "DV", dv)
        st = p.dma("sts%d" % (i % 2), s_scr[i * 128:(i + 1) * 128, :], eb)
        wo_ev[i] = (dv, st)

    def fin_q(i):
        par = i % 2
        t2b, ob = t2sl[i % 3], osl[par]
        if i == 0:
            p.wait("DV", "bc5")
            p.wait("DV", "PL")
        if i >= 2:
            assert fin[i - 2][3] is not None
            p.wait("SY", "DV", fin[i - 2][3])
            p.wait("SY", "PE", fin[i - 2][0])
        tokx = p.dma("ldx3%d" % par, bfB[:, 0:D] if par == 0 else bfA[:, 0:D],
                     xb[i * 128:(i + 1) * 128, :])
        p.wait("SY", "sts%d" % par, wo_ev[i][1])
        toks = p.dma("lds3%d" % par,
                     bfC[:, 0:D] if par == 0 else rr[:, 0:512].bitcast(BF16),
                     s_scr[i * 128:(i + 1) * 128, :])
        fin_ld[i] = (tokx, toks)
        for k in range(4):
            p.wait("DV", "ldt2%d_%d" % (i % 3, k), t2_ld[i][k])
        if i >= 2:
            p.wait("DV", "PE", fin[i - 2][0])
        p.DV("tensor_scalar", t2b.rearrange("p a b -> p (a b)"),
             t2b.rearrange("p a b -> p (a b)"),
             nrm[:, 41:42], C_RNE, ALU.mult, ALU.add)
        d2 = p.DV("tensor_scalar", ob.rearrange("p a b -> p (a b)"),
                  t2b.rearrange("p a b -> p (a b)"),
                  C_RNE, nrm[:, 40:41], ALU.subtract, ALU.mult)
        fin_d2[i] = d2
        if i + 3 < NTM:
            p.wait("PL", "DV", d2)
            ldt2_issue(i + 3)

    def fin_m(i):
        par = i % 2
        ob = osl[par]
        pd = P2 if par == 0 else P34
        if i == 0:
            for _k in range(6):
                p.wait("PE", "ldwdt%d" % _k)
        p.wait("PE", "DV", fin_d2[i])
        if i >= 2:
            p.wait("PE", "AC", fin[i - 2][1])
        pe = None
        for c in range(NCI):
            p.PE("matmul", pd[:, 0:512], ob[:, c, :], wddqT[:, c, 0:512],
                 start=(c == 0), stop=(c == NCI - 1))
            pe = p.PE("matmul", pd[:, 512:1024], ob[:, c, :],
                      wddqT[:, c, 512:1024],
                      start=(c == 0), stop=(c == NCI - 1))
        p.wait("AC", "PE", pe)
        ga = p.AC("activation", sA[:, 0:D] if par == 0 else sB[:, 0:D],
                  pd[:, 0:1024], AF.Silu)
        fin[i] = (pe, ga, None, None)

    def fin_combine(i):
        par = i % 2
        tokx, toks = fin_ld[i]
        ga = fin[i][1]
        p.wait("DV", "AC", ga)
        p.wait("DV", "ldx3%d" % par, tokx)
        if i >= 2:
            p.wait("DV", "sty%d" % par, fin[i - 2][2])
        p.wait("DV", "lds3%d" % par, toks)
        sb_t = bfC[:, 0:D] if par == 0 else rr[:, 0:512].bitcast(BF16)
        ga_t = sA[:, 0:D] if par == 0 else sB[:, 0:D]
        xb_t = bfB[:, 0:D] if par == 0 else bfA[:, 0:D]
        dcmb = sD if par == 0 else sE
        p.DV("scalar_tensor_tensor", dcmb[:, 0:512],
             ga_t[:, 0:512], 1.0, sb_t[:, 0:512], ALU.mult, ALU.mult)
        cm2 = rr[:, 512:1024] if par == 0 else sC[:, 0:512]
        p.DV("scalar_tensor_tensor", cm2,
             ga_t[:, 512:1024], 1.0, sb_t[:, 512:1024], ALU.mult, ALU.mult)
        p.DV("tensor_add", dcmb[:, 0:512], dcmb[:, 0:512], xb_t[:, 0:512])
        d4b = p.DV("tensor_add", cm2, cm2, xb_t[:, 512:1024])
        p.wait("SY", "DV", d4b)
        p.dma("sty%d" % par, y[i * 128:(i + 1) * 128, 0:512], dcmb[:, 0:512])
        st = p.dma("sty%d" % par, y[i * 128:(i + 1) * 128, 512:1024], cm2)
        fin[i] = (fin[i][0], fin[i][1], st, d4b)

    for it in range(NTM + 4):
        if it < NTM:
            wo_chunk(it)
        if 0 <= it - 4 < NTM:
            fin_combine(it - 4)
        if 0 <= it - 2 < NTM:
            fin_q(it - 2)
        if 0 <= it - 3 < NTM:
            fin_m(it - 3)
    for s in list(p.cnt.keys()):
        p.wait("SY", s)
    return nc


def _prep_inputs(cfg, inp):
    S, D, HD = cfg["S"], cfg["D"], cfg["HD"]
    TOK = cfg["TOK"]
    x = np.asarray(inp["L0_attn_input_embeds"])
    Wq = np.asarray(inp["L0_attn_Wq"]); bq = np.asarray(inp["L0_attn_bq"])
    Wk = np.asarray(inp["L0_attn_Wk"]); bk = np.asarray(inp["L0_attn_bk"])
    Wv = np.asarray(inp["L0_attn_Wv"]); bv = np.asarray(inp["L0_attn_bv"])
    Wo = np.asarray(inp["L0_attn_Wo"]); bo = np.asarray(inp["L0_attn_bo"])
    wg = np.ascontiguousarray(np.asarray(inp["L0_ffn_w_gate"]))
    wu = np.ascontiguousarray(np.asarray(inp["L0_ffn_w_up"]))
    wd = np.ascontiguousarray(np.asarray(inp["L0_ffn_w_down"]))
    wqkvT = np.ascontiguousarray(np.concatenate([Wq.T, Wk.T, Wv.T], axis=1))
    woT = np.ascontiguousarray(Wo.T)
    bqkv = np.ascontiguousarray(np.concatenate([bq, bk, bv]).reshape(1, 3 * D))
    bo_r = np.ascontiguousarray(bo.reshape(1, D))
    inv = (1.0 / (10000.0 ** (np.arange(0, HD, 2, dtype=np.float32) / HD))).astype(np.float32)
    t = np.arange(S, dtype=np.float32)
    fr = np.einsum("s,d->sd", t, inv).astype(np.float32)
    emb = np.concatenate([fr, fr], axis=1)
    cosb = np.cos(emb).astype(BF).astype(np.float32)
    sinb = np.sin(emb).astype(BF).astype(np.float32)
    sinb[:, 0:HD // 2] *= -1.0
    cosb = np.ascontiguousarray(cosb * np.float32(INV_SQRT8))
    sinb = np.ascontiguousarray(sinb * np.float32(INV_SQRT8))
    in_maps = []
    for c in range(8):
        b, half = c // 2, c % 2
        xbat = x[b]
        if half == 1:
            xb_c = np.ascontiguousarray(np.concatenate([xbat[TOK:], xbat[:TOK]], axis=0))
            tc = np.ascontiguousarray(np.concatenate([cosb[TOK:], cosb[:TOK]], axis=0))
            ts_ = np.ascontiguousarray(np.concatenate([sinb[TOK:], sinb[:TOK]], axis=0))
        else:
            xb_c = np.ascontiguousarray(xbat)
            tc, ts_ = cosb, sinb
        in_maps.append({
            "xb": xb_c,
            "wqkvT": wqkvT, "woT": woT, "bqkv": bqkv, "bo": bo_r,
            "wg": wg, "wu": wu, "wd": wd, "tabc": tc, "tabs": ts_,
        })
    return in_maps


def kernel(**inputs):
    cfg = mkcfg()
    in_maps = _prep_inputs(cfg, inputs)
    nc = build_core(cfg)
    res = run_bass_kernel_spmd(nc, in_maps, core_ids=list(range(8)))
    B, S, D, TOK = cfg["B"], cfg["S"], cfg["D"], cfg["TOK"]
    out = np.zeros((B, S, D), np.float32)
    for c in range(8):
        b, half = c // 2, c % 2
        out[b, half * TOK:(half + 1) * TOK] = res.results[c]["y"]
    return out
